# version 02 (final)
# baseline (speedup 1.0000x reference)
import numpy as np
import ml_dtypes
from contextlib import ExitStack

import concourse.bass as bass
import concourse.mybir as mybir
from concourse.bass_utils import run_bass_kernel_spmd

F32 = mybir.dt.float32
BF16 = mybir.dt.bfloat16
I32 = mybir.dt.int32
AF = mybir.ActivationFunctionType
ALU = mybir.AluOpType
AX = mybir.AxisListType
BF16NP = ml_dtypes.bfloat16

BATCH, SEQ, DM = 2, 4096, 4096
TLOC = 1024
NCORE, GRP = 8, 4
DIN, DCONV, DPROJ = 8192, 10240, 18560
NH, PDIM, NG, NST, CH = 128, 64, 8, 128, 256
NH1, NKV, HD1 = 32, 8, 128
EPS = 1e-5

ENGINES = ("sync", "gpsimd", "vector", "scalar", "tensor")

LOGC = -2.4260151319598084
SC6 = 0.08838834764831845
CONST_VALS = [EPS, -0.5, -1.0, 1.0 / DIN, 1.0 / DM, 0.5, LOGC, SC6]


class _Call:
    __slots__ = ("name", "args", "kwargs", "incs")

    def __init__(self, name, args, kwargs):
        self.name, self.args, self.kwargs = name, args, kwargs
        self.incs = []

    def then_inc(self, semh, n=1):
        self.incs.append((semh, n))
        return self


class EngQ:
    def __init__(self):
        self.items = []

    def __getattr__(self, name):
        if name.startswith("_"):
            raise AttributeError(name)

        def f(*a, **k):
            c = _Call(name, a, k)
            self.items.append(("call", c))
            return c

        return f

    def raw(self, fn, sim_incs=()):
        fn._sim_incs = list(sim_incs)
        self.items.append(("raw", fn))


class Sem:
    def __init__(self, h):
        self.h = h
        self.n = 0

    def inc(self, call, k=1):
        call.then_inc(self.h, k)
        self.n += k
        return self.n


class Ctx:

    def __init__(self, nc, ctx, mode_tensors, split=False):
        self.nc = nc
        self.split = split
        self.ctx = ctx
        self.q = {e: EngQ() for e in ENGINES}
        self.dram = {}
        self.sems = {}
        self._mt = mode_tensors
        self._bar = None
        self._barn = 0

    def sem(self, name):
        if name not in self.sems:
            h = self.ctx.enter_context(self.nc.semaphore(name))
            self.sems[name] = Sem(h)
        return self.sems[name]

    def sbuf(self, name, shape, dtype):
        t = self.ctx.enter_context(self.nc.sbuf_tensor(name, list(shape), dtype))
        return t

    def psum(self, name, shape, dtype=F32):
        t = self.ctx.enter_context(self.nc.psum_tensor(name, list(shape), dtype))
        return t

    def dram_t(self, name, shape, dtype):
        if name not in self.dram:
            kind = self._mt.get(name, "Internal")
            h = self.nc.dram_tensor(name, list(shape), dtype, kind=kind)
            self.dram[name] = h
        return self.dram[name]

    def chain_dma(self, q, sem, out, in_):
        if sem.n > 0:
            q.wait_ge(sem.h, sem.n)
        return sem.inc(q.dma_start(out=out, in_=in_), 16)

    def barrier(self, engines=ENGINES):
        if self._bar is None:
            self._bar = self.sem("barrier")
        self._barn += 1
        tgt = len(engines) * self._barn
        for e in engines:
            q = self.q[e]
            q.drain()
            q.sem_inc(self._bar.h, 1)
            q.wait_ge(self._bar.h, tgt)
        self._bar.n = tgt

    def simulate(self):
        ptr = {e: 0 for e in ENGINES}
        vals = {}
        items = {e: self.q[e].items for e in ENGINES}

        def blocked(e):
            it = items[e]
            i = ptr[e]
            if i >= len(it):
                return None
            kind, c = it[i]
            if kind == "call" and c.name in ("wait_ge",):
                sem, v = c.args[0], c.args[1]
                if vals.get(id(sem), 0) < v:
                    return (sem, v)
            return False

        progress = True
        while progress:
            progress = False
            for e in ENGINES:
                while True:
                    b = blocked(e)
                    if b is None or b:
                        break
                    kind, c = items[e][ptr[e]]
                    if kind == "call":
                        if c.name == "sem_inc":
                            vals[id(c.args[0])] = vals.get(id(c.args[0]), 0) + c.args[1]
                        for semh, n in c.incs:
                            vals[id(semh)] = vals.get(id(semh), 0) + n
                    else:
                        for semh, n in getattr(c, "_sim_incs", []):
                            vals[id(semh)] = vals.get(id(semh), 0) + n
                    ptr[e] += 1
                    progress = True
        stuck = {e: (ptr[e], len(items[e]), blocked(e)) for e in ENGINES
                 if ptr[e] < len(items[e])}
        if stuck:
            name_of = {id(s_.h): nm for nm, s_ in self.sems.items()}
            msg = []
            for e, (i, n, b) in stuck.items():
                if b:
                    sem, v = b
                    msg.append(f"{e}@{i}/{n} waits {name_of.get(id(sem), '?')}>={v} (cur {vals.get(id(sem), 0)})")
                else:
                    msg.append(f"{e}@{i}/{n} ???")
            raise RuntimeError("DEADLOCK in build-time sim:\n" + "\n".join(msg))

    def finish(self):
        nc = self.nc
        with nc.Block() as block:
            for e in ENGINES:
                items = self.q[e].items

                def mk(items):
                    def run(eng):
                        for kind, it in items:
                            if kind == "raw":
                                it(eng)
                            else:
                                inst = getattr(eng, it.name)(*it.args, **it.kwargs)
                                for s, n in it.incs:
                                    inst.then_inc(s, n)
                    return run

                getattr(block, e)(mk(items))


DRAM_SPEC = {
    "x0": ([DM, TLOC], F32),
    "wip": ([145, 128, 32, 128], F32),
    "qw_sh": ([64, 4096], I32),
    "ln_ones": ([1, 4], F32),
    "ZX": ([DPROJ - 128, TLOC], BF16),
    "DT": ([128, TLOC], F32),
    "HALO_in": ([128, 240], BF16),
    "LN0AB": ([1, 2 * TLOC], F32),
    "WDQ_in": ([512, 4096], BF16),
    "cw": ([128, 80, 4], F32),
    "cb": ([128, 80], F32),
    "dtb_col": ([128, 1], F32),
    "alog_col": ([128, 1], F32),
    "d_col": ([128, 1], F32),
    "nw_cols": ([128, 64], F32),
    "U01": ([2, 128, 256], F32),
    "GTMASK": ([2, 128, 256], BF16),
    "SPLIT2": ([2, 128], F32),
    "PARITY": ([128, 128], F32),
    "PAT2": ([128, 64], F32),
    "IDF32": ([128, 128], F32),
    "IDBF": ([128, 128], BF16),
    "ONES256F": ([1, 256], F32),
    "WMSKT": ([4, 4], F32),
    "WVALID": ([128, 4], F32),
    "HGATE": ([128, 1], F32),
    "DYNOFF": ([1, 8], I32),
    "XCPARK": ([DCONV, TLOC], BF16),
    "SLOC": ([4 * 128, DIN], BF16),
    "YD": ([DIN, TLOC], BF16),
    "ACS": ([128, TLOC], F32),
    "GLOG": ([128, 4], F32),
    "SST_in": ([128, DIN], BF16),
    "GAM_in": ([1, 128], F32),
    "SST_out": ([GRP * 128, DIN], BF16),
    "GAM_out": ([GRP, 128], F32),
    "YBUF": ([DIN, TLOC], BF16),
    "DBG": ([130, DIN], F32),
    "wot": ([8, 64, 128, 512], F32),
    "XRES": ([DM, TLOC], F32),
    "XRES2": ([DM, TLOC], F32),
    "routerT": ([32, 128, 16], F32),
    "AT": ([32, 128, 64], F32),
    "wbaseT": ([32, 128, 32, 128], F32),
    "BT4": ([32, 64, 2048], F32),
    "baseb": ([128, 32], F32),
    "CROW": ([16, TLOC], BF16),
    "wqkvT": ([48, 128, 32, 128], F32),
    "QROPE": ([4096, TLOC], BF16),
    "KV_in": ([2048, TLOC], BF16),
    "KV_out": ([GRP * 2048, TLOC], BF16),
    "Q_out": ([GRP * 4096, TLOC], BF16),
    "JROT": ([128, 128], BF16),
    "RCOS": ([128, TLOC], F32),
    "RSIN": ([128, TLOC], F32),
    "RMS": ([1, TLOC], F32),
    "O_in": ([1024, 4096], BF16),
    "O_out": ([4096, 4096], BF16),
    "DYN6": ([1, 160], I32),
    "MASKD": ([4, 128, 512], BF16),
    "wo2T": ([8, 32, 128, 512], F32),
    "XRES3": ([DM, TLOC], F32),
    "P7Z": ([1, 4096], F32),
    "P7S": ([1, 4096], F32),
    "P7SCOL": ([128, 32], F32),
    "P7BCOL": ([128, 32], F32),
    "OUT": ([DM, TLOC], F32),
    "XBFP": ([DM, TLOC], BF16),
    "STATP": ([2, 3 * TLOC], F32),
    "HALO_out": ([GRP * 128, 240], BF16),
    "WDQ_out": ([4096, 4096], BF16),
}

COLLECTIVES = {
    "HALO": ("HALO_in", "HALO_out", "batch"),
    "WDQ": ("WDQ_in", "WDQ_out", "all"),
    "SST": ("SST_in", "SST_out", "batch"),
    "GAM": ("GAM_in", "GAM_out", "batch"),
    "KV": ("KV_in", "KV_out", "batch"),
    "QG": ("QROPE", "Q_out", "batch"),
    "OG": ("O_in", "O_out", "batch"),
}


def _groups(kind):
    return [[0, 1, 2, 3], [4, 5, 6, 7]] if kind == "batch" else [[0, 1, 2, 3, 4, 5, 6, 7]]


def emit_collective(C, name):
    if C.split:
        return None
    gin, gout, kind = COLLECTIVES[name]
    hin = C.dram[gin]
    hout = C.dram_t(gout, *DRAM_SPEC[gout])
    s = C.sem("cc_" + name)
    g = C.q["gpsimd"]
    call = g.collective_compute(
        "AllGather", ALU.bypass, replica_groups=_groups(kind),
        ins=[hin.ap()], outs=[hout.ap()])
    s.inc(call, 1)
    return s


def emit_p1(C, G):
    nc = C.nc
    qs, qg, qv, qc, qt = (C.q[e] for e in ENGINES)

    x0 = C.dram_t("x0", *DRAM_SPEC["x0"]).ap()
    wip = C.dram_t("wip", *DRAM_SPEC["wip"]).ap()
    ZX = C.dram_t("ZX", *DRAM_SPEC["ZX"]).ap()
    DTd = C.dram_t("DT", *DRAM_SPEC["DT"]).ap()
    HALO_in = C.dram_t("HALO_in", *DRAM_SPEC["HALO_in"]).ap()

    XBF = C.sbuf("p1_xbf", [128, 32 * TLOC], BF16)
    ones_col = G["ones_col"]
    ones_row_f = G["ones_row_f"]

    xf = C.sbuf("p1_xf", [128, 2 * TLOC], F32)
    xsq = C.sbuf("p1_xsq", [128, 2 * TLOC], BF16)
    wsb = C.sbuf("p1_w", [128, 2 * 4096], BF16)
    zxsb = C.sbuf("p1_zx", [128, 2 * TLOC], BF16)
    halo = C.sbuf("p1_halo", [128, 240], BF16)
    stat = C.sbuf("p1_stat", [1, 8 * TLOC], F32)
    ab = C.sbuf("p1_ab", [128, 2 * TLOC], F32)
    dtsb = C.sbuf("p1_dt", [128, TLOC], F32)

    psM = C.psum("p1_psM", [128, 4 * TLOC])
    psS = psM[0:1, 0:2 * TLOC]

    s_xld = [C.sem("p1_xld0"), C.sem("p1_xld1")]
    s_cast = C.sem("p1_cast")
    s_sq = C.sem("p1_sq")
    s_ab = C.sem("p1_ab")
    s_ln = C.sem("p1_ln")
    s_wld = [C.sem("p1_wld0"), C.sem("p1_wld1")]
    s_pe = C.sem("p1_pe")
    s_ev = C.sem("p1_ev")
    s_zst = [C.sem("p1_zst0"), C.sem("p1_zst1")]
    s_dts = C.sem("p1_dts")
    s_dq = C.sem("p1_dq")
    s_qld = C.sem("p1_qld")

    s_stmm = C.sem("p1_stmm")
    for c in range(32):
        sl = (c % 2) * TLOC
        if c >= 2:
            qs.wait_ge(s_cast.h, c - 1)
        v = s_xld[c % 2].inc(qs.dma_start(out=xf[:, sl:sl + TLOC], in_=x0[c * 128:(c + 1) * 128, :]), 16)
        qv.wait_ge(s_xld[c % 2].h, v)
        s_cast.inc(qv.tensor_copy(XBF[:, c * TLOC:(c + 1) * TLOC], xf[:, sl:sl + TLOC]), 1)
        qc.wait_ge(s_cast.h, c + 1)
        if c >= 2:
            qc.wait_ge(s_stmm.h, c - 1)
        s_sq.inc(qc.activation(xsq[:, sl:sl + TLOC], XBF[:, c * TLOC:(c + 1) * TLOC], AF.Square), 1)
        qt.wait_ge(s_cast.h, c + 1)
        qt.wait_ge(s_sq.h, c + 1)
        first = (c == 0)
        for th in range(2):
            qt.matmul(psS[0:1, th * 512:(th + 1) * 512],
                      ones_col[:], XBF[:, c * TLOC + th * 512: c * TLOC + (th + 1) * 512],
                      start=first, stop=(c == 31))
        for th in range(2):
            call = qt.matmul(psS[0:1, TLOC + th * 512: TLOC + (th + 1) * 512],
                             ones_col[:], xsq[:, sl + th * 512: sl + (th + 1) * 512],
                             start=first, stop=(c == 31))
        s_stmm.inc(call, 1)

    qv.wait_ge(s_stmm.h, 32)
    mrow = stat[0:1, 0:TLOC]
    vrow = stat[0:1, TLOC:2 * TLOC]
    arow = stat[0:1, 2 * TLOC:3 * TLOC]
    brow = stat[0:1, 3 * TLOC:4 * TLOC]
    qv.tensor_scalar_mul(mrow, psS[0:1, 0:TLOC], 1.0 / DM)
    qv.tensor_scalar_mul(vrow, psS[0:1, TLOC:2 * TLOC], 1.0 / DM)
    t0 = stat[0:1, 4 * TLOC:5 * TLOC]
    qv.tensor_tensor(t0, mrow, mrow, ALU.mult)
    s_ab.inc(qv.tensor_tensor(vrow, vrow, t0, ALU.subtract), 1)
    qc.wait_ge(s_ab.h, 1)
    qc.activation(t0, vrow, AF.Ln, bias=EPS)
    s_ab.inc(qc.activation(arow, t0, AF.Exp, scale=-0.5), 1)
    qv.wait_ge(s_ab.h, 2)
    qv.tensor_tensor(brow, mrow, arow, ALU.mult)
    s_ab.inc(qv.tensor_scalar_mul(brow, brow, -1.0), 1)
    qt.wait_ge(s_ab.h, 3)
    for th in range(2):
        qt.matmul(psM[:, 2 * TLOC + th * 512: 2 * TLOC + (th + 1) * 512],
                  ones_row_f[:], arow[0:1, th * 512:(th + 1) * 512], start=True, stop=True)
    for th in range(2):
        call = qt.matmul(psM[:, 3 * TLOC + th * 512: 3 * TLOC + (th + 1) * 512],
                         ones_row_f[:], brow[0:1, th * 512:(th + 1) * 512], start=True, stop=True)
    s_ab.inc(call, 1)
    qv.wait_ge(s_ab.h, 4)
    qv.tensor_copy(ab[:, 0:TLOC], psM[:, 2 * TLOC:3 * TLOC])
    s_ab.inc(qv.tensor_copy(ab[:, TLOC:2 * TLOC], psM[:, 3 * TLOC:4 * TLOC]), 1)
    LN0AB = C.dram_t("LN0AB", *DRAM_SPEC["LN0AB"]).ap()
    qs.wait_ge(s_ab.h, 3)
    s_dts.inc(qs.dma_start(out=LN0AB, in_=stat[0:1, 2 * TLOC:4 * TLOC]), 16)

    tmp = xsq
    for c in range(32):
        sl = (c % 2) * TLOC
        qv.tensor_tensor(tmp[:, sl:sl + TLOC], XBF[:, c * TLOC:(c + 1) * TLOC], ab[:, 0:TLOC], ALU.mult)
        call = qv.tensor_tensor(XBF[:, c * TLOC:(c + 1) * TLOC], tmp[:, sl:sl + TLOC], ab[:, TLOC:2 * TLOC], ALU.add)
    s_ln.inc(call, 1)

    qt.wait_ge(s_ln.h, 1)
    evn = {}
    stn = {}
    for f in range(145):
        wslot = (f % 2) * 4096
        if f >= 2:
            qg.wait_ge(s_pe.h, f - 1)
        v = s_wld[f % 2].inc(qg.dma_start(out=wsb[:, wslot:wslot + 4096], in_=wip[f]), 16)
        qt.wait_ge(s_wld[f % 2].h, v)
        pslot = (f % 2) * TLOC
        if f >= 2:
            qt.wait_ge(s_ev.h, evn[f - 2])
        for dc in range(32):
            for th in range(2):
                call = qt.matmul(psM[:, pslot + th * 512: pslot + (th + 1) * 512],
                                 wsb[:, wslot + dc * 128: wslot + (dc + 1) * 128],
                                 XBF[:, dc * TLOC + th * 512: dc * TLOC + (th + 1) * 512],
                                 start=(dc == 0), stop=(dc == 31))
        s_pe.inc(call, 1)
        zslot = (f % 2) * TLOC
        qv.wait_ge(s_pe.h, f + 1)
        if f >= 2 and f != 144:
            qv.wait_ge(s_zst[f % 2].h, stn[f - 2])
        if f == 144:
            ev = s_ev.inc(qv.tensor_copy(dtsb[:], psM[:, pslot:pslot + TLOC]), 1)
            evn[f] = ev
            qs.wait_ge(s_ev.h, ev)
            stn[f] = s_dts.inc(qs.dma_start(out=DTd, in_=dtsb[:]), 16)
        else:
            ev = s_ev.inc(qv.tensor_copy(zxsb[:, zslot:zslot + TLOC], psM[:, pslot:pslot + TLOC]), 1)
            if f >= 64:
                ev = s_ev.inc(qv.tensor_copy(halo[:, (f - 64) * 3:(f - 63) * 3],
                                             psM[:, pslot + 1021:pslot + 1024]), 1)
            evn[f] = ev
            qs.wait_ge(s_ev.h, ev)
            stn[f] = s_zst[f % 2].inc(qs.dma_start(out=ZX[f * 128:(f + 1) * 128, :], in_=zxsb[:, zslot:zslot + TLOC]), 16)

    hst = s_dts.inc(qs.dma_start(out=HALO_in, in_=halo[:]), 16)

    qw = C.dram_t("qw_sh", *DRAM_SPEC["qw_sh"]).ap()
    WDQ_in = C.dram_t("WDQ_in", *DRAM_SPEC["WDQ_in"]).ap()
    qwsb = C.sbuf("p1_qw", [64, 4096], I32)
    nib = C.sbuf("p1_nib", [64, 4096], I32)
    nibbf = C.sbuf("p1_nibbf", [64, 4096], BF16)
    v = s_qld.inc(qs.dma_start(out=qwsb[:], in_=qw), 16)
    qv.wait_ge(s_qld.h, v)
    s_dqc = C.sem("p1_dqc")
    wdq_view = WDQ_in.rearrange("(iw j) o -> iw j o", j=8)
    for j in range(8):
        qv.tensor_scalar(nib[:], qwsb[:], 4 * j, 15, ALU.logical_shift_right, ALU.bitwise_and)
        cnv = s_dqc.inc(qv.tensor_copy(nibbf[:], nib[:]), 1)
        qs.wait_ge(s_dqc.h, cnv)
        v = s_dq.inc(qs.dma_start(out=wdq_view[:, j, :], in_=nibbf[:]), 16)
        if j < 7:
            qv.wait_ge(s_dq.h, v)

    qs.wait_ge(s_zst[0].h, s_zst[0].n)
    qs.wait_ge(s_zst[1].h, s_zst[1].n)
    qs.wait_ge(s_dts.h, s_dts.n)
    qs.wait_ge(s_dq.h, s_dq.n)
    qg.wait_ge(s_zst[0].h, s_zst[0].n)
    qg.wait_ge(s_zst[1].h, s_zst[1].n)
    qg.wait_ge(s_dts.h, s_dts.n)
    qg.wait_ge(s_dq.h, s_dq.n)
    emit_collective(C, "HALO")
    emit_collective(C, "WDQ")


P2A_STAGE = 3

def emit_p2a(C, G):
    qs, qg, qv, qc, qt = (C.q[e] for e in ENGINES)
    D = DRAM_SPEC

    ZX = C.dram_t("ZX", *D["ZX"]).ap()
    DTd = C.dram_t("DT", *D["DT"]).ap()
    HALO_out_h = C.dram_t("HALO_out", *D["HALO_out"])
    XCPARK = C.dram_t("XCPARK", *D["XCPARK"]).ap()
    SLOC = C.dram_t("SLOC", *D["SLOC"]).ap()
    YD = C.dram_t("YD", *D["YD"]).ap()
    ACSd = C.dram_t("ACS", *D["ACS"]).ap()
    GLOGd = C.dram_t("GLOG", *D["GLOG"]).ap()
    SST_in = C.dram_t("SST_in", *D["SST_in"]).ap()
    GAM_in = C.dram_t("GAM_in", *D["GAM_in"]).ap()
    cw = C.dram_t("cw", *D["cw"]).ap()
    cb = C.dram_t("cb", *D["cb"]).ap()
    dtb = C.dram_t("dtb_col", *D["dtb_col"]).ap()
    alog = C.dram_t("alog_col", *D["alog_col"]).ap()
    U01 = C.dram_t("U01", *D["U01"]).ap()
    GTM = C.dram_t("GTMASK", *D["GTMASK"]).ap()
    PARITY = C.dram_t("PARITY", *D["PARITY"]).ap()
    PAT2 = C.dram_t("PAT2", *D["PAT2"]).ap()
    IDF32 = C.dram_t("IDF32", *D["IDF32"]).ap()
    IDBF = C.dram_t("IDBF", *D["IDBF"]).ap()
    HGATE = C.dram_t("HGATE", *D["HGATE"]).ap()
    DYNOFF = C.dram_t("DYNOFF", *D["DYNOFF"]).ap()
    DCOL = C.dram_t("d_col", *D["d_col"]).ap()

    ones_row_f = G["ones_row_f"]

    convin = C.sbuf("p2_convin", [128, 2 * 10 * 259], BF16)
    convacc = C.sbuf("p2_convacc", [128, 2 * 10 * 256], F32)
    XBCc = C.sbuf("p2_xbcc", [128, 80 * 256], BF16)
    xtok = C.sbuf("p2_xtok", [128, 2 * 8192], BF16)
    dtv = C.sbuf("p2_dtv", [128, TLOC], F32)
    dA = C.sbuf("p2_dA", [128, TLOC], F32)
    acs = C.sbuf("p2_acs", [128, TLOC], F32)
    dAtok = C.sbuf("p2_dAtok", [128, 256], F32)
    dtvtok = C.sbuf("p2_dtvtok", [128, 256], F32)
    wdk = C.sbuf("p2_wdk", [128, 256], F32)
    wcomb = C.sbuf("p2_wcomb", [128, 256], F32)
    wctok = C.sbuf("p2_wctok", [128, 256], F32)
    natok = C.sbuf("p2_natok", [128, 256], F32)
    rows = C.sbuf("p2_rows", [1, 2 * 2048], F32)
    gtm = C.sbuf("p2_gtm", [128, 16 * 256], BF16)
    btok = C.sbuf("p2_btok", [128, 16 * 128], BF16)
    xdtb = C.sbuf("p2_xdtb", [128, 8 * 64], BF16)
    xdwb = C.sbuf("p2_xdwb", [128, 8 * 64], BF16)
    Ebuf = C.sbuf("p2_E", [128, 4 * 256], BF16)
    Mbuf = C.sbuf("p2_M", [128, 4 * 256], BF16)
    Srun = C.sbuf("p2_srun", [128, DIN], BF16)
    slocbf = C.sbuf("p2_slocbf", [128, DIN], BF16)
    ydsb = C.sbuf("p2_ydsb", [128, 2 * 256], BF16)
    grep = C.sbuf("p2_grep", [128, 64], F32)
    gam4 = C.sbuf("p2_gam4", [128, 4], F32)
    negGl = C.sbuf("p2_negGl", [128, 1], F32)
    gtot = C.sbuf("p2_gtot", [128, 1], F32)
    dyn = C.sbuf("p2_dyn", [1, 8], I32)
    cwsb = C.sbuf("p2_cw", [128, 320], F32)
    cbsb = C.sbuf("p2_cb", [128, 80], F32)
    hg = C.sbuf("p2_hg", [128, 1], F32)
    pt2 = C.sbuf("p2_pt2", [128, 64], F32)
    parity = C.sbuf("p2_parity", [128, 128], F32)
    idf = C.sbuf("p2_idf", [128, 128], F32)
    idb = C.sbuf("p2_idb", [128, 128], BF16)
    u01 = C.sbuf("p2_u01", [128, 512], F32)
    gtmask = C.sbuf("p2_gtmask", [128, 512], BF16)
    dcolp = C.sbuf("p2_dcol", [128, 1], F32)
    dbc = C.sbuf("p2_dbc", [128, 64], F32)
    tcol = C.sbuf("p2_tcol", [128, 1], F32)
    dtbc = C.sbuf("p2_dtbc", [128, 1], F32)
    alsb = C.sbuf("p2_alsb", [128, 1], F32)

    psY = C.psum("p2_psY", [128, 1024])
    psS = C.psum("p2_psS", [128, 1024])
    psD = C.psum("p2_psD", [128, 1024])
    psK = C.psum("p2_psK", [128, 512])
    psT = C.psum("p2_psT", [128, 1024], BF16)

    s_ld = C.sem("p2_ld")
    s_cv = [C.sem("p2_cv0"), C.sem("p2_cv1")]
    s_cg = C.sem("p2_cg")
    s_sil = C.sem("p2_sil")
    s_k2t = C.sem("p2_k2t")
    s_k2v = C.sem("p2_k2v")
    s_k2c = C.sem("p2_k2c")
    s_bc = C.sem("p2_bc")
    s_E = C.sem("p2_E")
    s_M = C.sem("p2_M")
    s_xd = C.sem("p2_xd")
    s_Y = C.sem("p2_Y")
    s_ev = C.sem("p2_ev")
    s_rg = [C.sem("p2_rg0"), C.sem("p2_rg1")]
    s_pxc = C.sem("p2_pxc")
    s_pacs = [C.sem("p2_pacs0"), C.sem("p2_pacs1")]
    s_psloc = C.sem("p2_psloc")
    s_yd = [C.sem("p2_yd0"), C.sem("p2_yd1")]
    s_pend = C.sem("p2_pend")

    evh, exh, mvh, xdh = {}, {}, {}, {}
    evhp, ydst = {}, {}
    xcparkn = {}

    for ap_, sb_ in ((cw.rearrange("p a b -> p (a b)"), cwsb[:]), (cb, cbsb[:]),
                     (HGATE, hg[:]), (PAT2, pt2[:]), (PARITY, parity[:]),
                     (IDF32, idf[:]), (IDBF, idb[:]),
                     (U01.rearrange("a p b -> p a b"), u01[:].rearrange("p (a b) -> p a b", b=256)),
                     (GTM.rearrange("a p b -> p a b"), gtmask[:].rearrange("p (a b) -> p a b", b=256)),
                     (DYNOFF, dyn[:]), (DCOL, dcolp[:]),
                     (dtb, dtbc[:]), (alog, alsb[:])):
        C.chain_dma(qs, s_ld, sb_, ap_)
    C.chain_dma(qs, s_ld, convacc[:, 0:TLOC], DTd)
    qv.wait_ge(s_ld.h, s_ld.n)
    qt.wait_ge(s_ld.h, s_ld.n)
    qc.wait_ge(s_ld.h, s_ld.n)
    dtr = convacc[:, 0:TLOC]
    t1 = convacc[:, TLOC:2 * TLOC]
    t2 = convacc[:, 2 * TLOC:3 * TLOC]
    s_k2v.inc(qv.tensor_scalar_add(dtr, dtr, dtbc[:, 0:1]), 1)
    qc.wait_ge(s_k2v.h, s_k2v.n)
    s_k2c.inc(qc.activation(t1, dtr, AF.Abs), 1)
    s_k2c.inc(qc.activation(t2, t1, AF.Exp, scale=-1.0), 1)
    qv.wait_ge(s_k2c.h, s_k2c.n)
    s_k2v.inc(qv.tensor_single_scalar(t1, dtr, 0.0, ALU.max), 1)
    qc.wait_ge(s_k2v.h, s_k2v.n)
    s_k2c.inc(qc.activation(t2, t2, AF.Ln, bias=1.0), 1)
    s_k2c.inc(qc.activation(tcol[:], alsb[:], AF.Exp), 1)
    qv.wait_ge(s_k2c.h, s_k2c.n)
    qv.tensor_scalar_mul(tcol[:], tcol[:], -1.0)
    qv.tensor_tensor(dtv[:], t1, t2, ALU.add)
    s_k2v.inc(qv.tensor_scalar_mul(dA[:], dtv[:], tcol[:, 0:1]), 1)
    s_k2v.inc(qv.tensor_scalar_mul(grep[:], pt2[:], dcolp[:, 0:1]), 1)
    qt.wait_ge(s_k2v.h, s_k2v.n)
    s_k2t.inc(qt.matmul(psK[:, 0:64], parity[:], grep[:], start=True, stop=True), 1)
    qv.wait_ge(s_k2t.h, s_k2t.n)
    s_k2v.inc(qv.tensor_copy(dbc[:], psK[:, 0:64]), 1)

    ccH = C.sems.get("cc_HALO")
    zx_v = ZX.rearrange("(a p) t -> p a t", p=128)
    cv3 = convin[:].rearrange("p (a t) -> p a t", t=259)

    for k in range(4):
        t0 = k * 256
        gh0 = k * NH
        hp0 = k * 64
        for gg in range(8):
            csl = (gg % 2) * 10
            ngrp = 8 * k + gg
            sld = s_cv[gg % 2]
            if ngrp >= 2:
                qs.wait_ge(s_cg.h, ngrp - 1)
            if k == 0:
                if gg == 0 and ccH is not None:
                    qs.wait_ge(ccH.h, 1)
                sld.inc(qs.dma_start(out=cv3[:, csl:csl + 10, 3:259],
                                     in_=zx_v[:, 64 + gg * 10:64 + gg * 10 + 10, 0:256]), 16)

                def halo_dma(eng, dst=cv3[:, csl:csl + 10, 0:3], src_t=HALO_out_h,
                             dynsb=dyn, semh=sld.h, gg=gg, w=sld.n):
                    with eng.register(f"hoff{gg}") as r:
                        eng.reg_load(r, dynsb[:1, gg:gg + 1])
                        eng.wait_ge(semh, w)
                        eng.dma_start(out=dst, in_=bass.AP(src_t, r, [[240, 128], [3, 10], [1, 3]])).then_inc(semh, 16)
                qs.raw(halo_dma, sim_incs=[(sld.h, 16)])
                sld.n += 16
            else:
                sld.inc(qs.dma_start(out=cv3[:, csl:csl + 10, :],
                                     in_=zx_v[:, 64 + gg * 10:64 + gg * 10 + 10, t0 - 3:t0 + 256]), 16)
            ldc = sld.n
            qv.wait_ge(sld.h, ldc)
            if k == 0:
                s_k2v.inc(qv.tensor_scalar_mul(cv3[:, csl:csl + 10, 0:3], cv3[:, csl:csl + 10, 0:3], hg[:, 0:1]), 1)
            ccslot = (gg % 2) * 2560
            if ngrp >= 2:
                qv.wait_ge(s_sil.h, ngrp - 1)
            for fi in range(10):
                fc = gg * 10 + fi
                w4 = cwsb[:, fc * 4:fc * 4 + 4]
                o = convacc[:, ccslot + fi * 256: ccslot + (fi + 1) * 256]
                qv.tensor_scalar(o, cv3[:, csl + fi, 0:256], w4[:, 0:1], cbsb[:, fc:fc + 1], ALU.mult, ALU.add)
                for j in (1, 2, 3):
                    call = qv.scalar_tensor_tensor(o, cv3[:, csl + fi, j:j + 256], w4[:, j:j + 1], o, ALU.mult, ALU.add)
            s_cg.inc(call, 1)
            qc.wait_ge(s_cg.h, ngrp + 1)
            if k > 0 and gg == 0:
                qc.wait_ge(s_ev.h, s_ev.n)
                qc.wait_ge(s_pxc.h, xcparkn[k - 1])
            s_sil.inc(qc.activation(XBCc[:, gg * 2560:(gg + 1) * 2560],
                                    convacc[:, ccslot:ccslot + 2560], AF.Silu), 1)
        qg.wait_ge(s_sil.h, 8 * (k + 1))
        xcparkn[k] = s_pxc.inc(qg.dma_start(
            out=XCPARK.rearrange("(a p) t -> p a t", p=128)[:, :, t0:t0 + 256],
            in_=XBCc[:].rearrange("p (a t) -> p a t", t=256)), 16)

        if P2A_STAGE < 2:
            continue
        qt.wait_ge(s_k2v.h, s_k2v.n)
        for sh in range(2):
            call = qt.transpose(psK[:, sh * 128:(sh + 1) * 128], dA[:, t0 + sh * 128: t0 + (sh + 1) * 128], idf[:])
        if P2A_STAGE < 2.05:
            s_k2t.inc(call, 1)
            continue
        s_k2t.inc(call, 1)
        qv.wait_ge(s_k2t.h, s_k2t.n)
        s_k2v.inc(qv.tensor_copy(dAtok[:], psK[:, 0:256]), 1)
        qt.wait_ge(s_k2v.h, s_k2v.n)
        qt.matmul(psK[:, 256:512], dAtok[:, 0:128], u01[:, 0:256], start=True, stop=False)
        s_k2t.inc(qt.matmul(psK[:, 256:512], dAtok[:, 128:256], u01[:, 256:512], start=False, stop=True), 1)
        qv.wait_ge(s_k2t.h, s_k2t.n)
        s_k2v.inc(qv.tensor_copy(acs[:, t0:t0 + 256], psK[:, 256:512]), 1)
        s_k2v.inc(qv.tensor_copy(gam4[:, k:k + 1], psK[:, 511:512]), 1)
        if P2A_STAGE < 2.1:
            continue
        qc.wait_ge(s_k2v.h, s_k2v.n)
        s_k2c.inc(qc.activation(wdk[:], acs[:, t0:t0 + 256], AF.Exp,
                                bias=gam4[:, k:k + 1], scale=-1.0), 1)
        qv.wait_ge(s_k2c.h, s_k2c.n)
        s_k2v.inc(qv.tensor_tensor(wcomb[:], wdk[:], dtv[:, t0:t0 + 256], ALU.mult), 1)
        qt.wait_ge(s_k2v.h, s_k2v.n)
        for src, dst in ((acs[:, t0:t0 + 256], natok),
                         (dtv[:, t0:t0 + 256], dtvtok),
                         (wcomb[:, 0:256], wctok)):
            for sh in range(2):
                call = qt.transpose(psK[:, sh * 128:(sh + 1) * 128], src[:, sh * 128:(sh + 1) * 128], idf[:])
            s_k2t.inc(call, 1)
            qv.wait_ge(s_k2t.h, s_k2t.n)
            s_k2v.inc(qv.tensor_copy(dst[:], psK[:, 0:256]), 1)
            qt.wait_ge(s_k2v.h, s_k2v.n)
        if P2A_STAGE < 2.2:
            continue
        qg.wait_ge(s_k2v.h, s_k2v.n)
        if k >= 2:
            qg.wait_ge(s_bc.h, NH * (k - 1))
        apark = s_pacs[k % 2].inc(qg.dma_start(out=ACSd[:, t0:t0 + 256], in_=acs[:, t0:t0 + 256]), 16)
        if P2A_STAGE < 2.3:
            continue
        qt.wait_ge(s_sil.h, 8 * (k + 1))
        nbt = 0
        for g in range(8):
            for sh in range(2):
                call = qt.transpose(psT[:, (nbt % 8) * 128:((nbt % 8) + 1) * 128],
                                    XBCc[:, (64 + g) * 256 + sh * 128:(64 + g) * 256 + (sh + 1) * 128], idb[:])
                nbt += 1
                if nbt % 8 == 0:
                    s_k2t.inc(call, 1)
                    qv.wait_ge(s_k2t.h, s_k2t.n)
                    s_k2v.inc(qv.tensor_copy(btok[:, (nbt - 8) * 128:nbt * 128], psT[:, 0:1024]), 1)
                    qt.wait_ge(s_k2v.h, s_k2v.n)
        for grp8 in range(16):
            for i8 in range(8):
                fidx = grp8 * 8 + i8
                sh, fc = fidx // 64, fidx % 64
                call = qt.transpose(psT[:, i8 * 128:(i8 + 1) * 128],
                                    XBCc[:, fc * 256 + sh * 128: fc * 256 + (sh + 1) * 128], idb[:])
            s_k2t.inc(call, 1)
            qv.wait_ge(s_k2t.h, s_k2t.n)
            s_k2v.inc(qv.tensor_copy(xtok[:, grp8 * 1024:(grp8 + 1) * 1024], psT[:, 0:1024]), 1)
            qt.wait_ge(s_k2v.h, s_k2v.n)
        if P2A_STAGE < 2.4:
            continue
        qt.wait_ge(s_E.h, s_E.n)
        gtev = {}
        for g in range(8):
            sl = (g % 2) * 512
            if g >= 2:
                qt.wait_ge(s_k2v.h, gtev[g - 2])
            qt.matmul(psD[:, sl:sl + 256],
                      XBCc[:, (64 + g) * 256:(64 + g) * 256 + 128],
                      XBCc[:, (72 + g) * 256:(72 + g) * 256 + 256], start=True, stop=False)
            s_k2t.inc(qt.matmul(psD[:, sl + 256:sl + 512],
                                XBCc[:, (64 + g) * 256 + 128:(64 + g) * 256 + 256],
                                XBCc[:, (72 + g) * 256:(72 + g) * 256 + 256], start=False, stop=True), 1)
            qv.wait_ge(s_k2t.h, s_k2t.n)
            qv.tensor_tensor(gtm[:, (2 * g) * 256:(2 * g + 1) * 256], psD[:, sl:sl + 256], gtmask[:, 0:256], ALU.mult)
            call = qv.tensor_tensor(gtm[:, (2 * g + 1) * 256:(2 * g + 2) * 256], psD[:, sl + 256:sl + 512], gtmask[:, 256:512], ALU.mult)
            gtev[g] = s_k2v.inc(call, 1)
        if P2A_STAGE < 2.5:
            continue
        s_k2c.inc(qc.activation(gtot[:], gam4[:, k:k + 1], AF.Exp), 1)
        qc.wait_ge(s_k2c.h, s_k2c.n)
        qv.wait_ge(s_k2c.h, s_k2c.n)
        s_k2v.inc(qv.tensor_scalar_mul(grep[:], pt2[:], gtot[:, 0:1]), 1)
        qt.wait_ge(s_k2v.h, s_k2v.n)
        s_k2t.inc(qt.matmul(psK[:, 0:64], parity[:], grep[:], start=True, stop=True), 1)
        qv.wait_ge(s_k2t.h, s_k2t.n)
        s_k2v.inc(qv.tensor_copy(grep[:], psK[:, 0:64]), 1)
        qv.wait_ge(s_k2v.h, s_k2v.n)
        k2v_done = s_k2v.n

        if P2A_STAGE < 3:
            continue
        qt.wait_ge(s_k2v.h, k2v_done)
        qc.wait_ge(s_k2v.h, k2v_done)

        def rows_block(hb, k=k, t0=t0, apark=apark):
            slot = (hb % 2) * 2048
            sr = s_rg[hb % 2]
            qs.wait_ge(s_pacs[k % 2].h, apark)
            if 16 * k + hb >= 2:
                qs.wait_ge(s_bc.h, 128 * k + 8 * (hb - 1))
            sr.inc(qs.dma_start(out=rows[0:1, slot:slot + 1024].rearrange("a (p t) -> a p t", t=256),
                                in_=ACSd[4 * hb:4 * hb + 4, t0:t0 + 256]), 16)
            sr.inc(qs.dma_start(out=rows[0:1, slot + 1024:slot + 2048].rearrange("a (p t) -> a p t", t=256),
                                in_=ACSd[64 + 4 * hb:64 + 4 * hb + 4, t0:t0 + 256]), 16)
            return sr.n

        rgn = {}
        rgn[0] = rows_block(0)
        rgn[1] = rows_block(1)

        def emit_xd(h, k=k, gh0=gh0):
            sl = (h % 4) * 128
            p = (h // 2) if h % 2 == 0 else 64 + h // 2
            gh = gh0 + h
            if gh >= 4:
                qv.wait_ge(s_Y.h, evh[gh - 4])
            fc, ph = h // 2, (h % 2) * 64
            for sh in range(2):
                qv.tensor_scalar_mul(xdtb[:, sl + sh * 64: sl + sh * 64 + 64],
                                     xtok[:, sh * 8192 + fc * 128 + ph: sh * 8192 + fc * 128 + ph + 64],
                                     dtvtok[:, sh * 128 + p: sh * 128 + p + 1])
                call = qv.tensor_scalar_mul(xdwb[:, sl + sh * 64: sl + sh * 64 + 64],
                                            xtok[:, sh * 8192 + fc * 128 + ph: sh * 8192 + fc * 128 + ph + 64],
                                            wctok[:, sh * 128 + p: sh * 128 + p + 1])
            xdh[gh] = s_xd.inc(call, 1)

        emit_xd(0)
        emit_xd(1)
        for h in range(NH):
            gh = gh0 + h
            p = (h // 2) if h % 2 == 0 else 64 + h // 2
            g = h // 16
            hp = h // 2
            ghp = hp0 + hp
            if gh >= 2:
                qt.wait_ge(s_E.h, exh[gh - 2])
            hb, r = h // 8, h % 8
            slot = (hb % 2) * 2048 + ((r % 2) * 1024) + (r // 2) * 256
            qt.wait_ge(s_rg[hb % 2].h, rgn[hb])
            call = qt.matmul(psD[:, (h % 2) * 512:(h % 2) * 512 + 256],
                             ones_row_f[:], rows[0:1, slot:slot + 256], start=True, stop=True)
            s_bc.inc(call, 1)
            qc.wait_ge(s_bc.h, s_bc.n)
            if gh >= 2:
                qc.wait_ge(s_M.h, mvh[gh - 2])
            for sh in range(2):
                esl = Ebuf[:, ((h % 2) * 2 + sh) * 256:((h % 2) * 2 + sh + 1) * 256]
                qc.activation(esl, psD[:, (h % 2) * 512:(h % 2) * 512 + 256], AF.Relu,
                              bias=natok[:, sh * 128 + p: sh * 128 + p + 1], scale=-1.0)
                call = qc.activation(esl, esl, AF.Exp, scale=-1.0)
            exh[gh] = s_E.inc(call, 1)
            qv.wait_ge(s_E.h, exh[gh])
            for sh in range(2):
                call = qv.scalar_tensor_tensor(Mbuf[:, ((h % 2) * 2 + sh) * 256:((h % 2) * 2 + sh + 1) * 256],
                                               Ebuf[:, ((h % 2) * 2 + sh) * 256:((h % 2) * 2 + sh + 1) * 256],
                                               1.0, gtm[:, (2 * g + sh) * 256:(2 * g + sh + 1) * 256],
                                               ALU.min, ALU.mult)
            mvh[gh] = s_M.inc(call, 1)
            if h + 2 < NH:
                emit_xd(h + 2)
            qt.wait_ge(s_M.h, mvh[gh])
            qt.wait_ge(s_xd.h, xdh[gh])
            if ghp >= 2 and h % 2 == 0:
                qt.wait_ge(s_ev.h, evhp[ghp - 2])
            ysl = (hp % 2) * 512
            pb = (h % 2) * 64
            sl = (h % 4) * 128
            for sh in range(2):
                qt.matmul(psY[pb:pb + 64, ysl:ysl + 256], xdtb[:, sl + sh * 64: sl + sh * 64 + 64],
                          Mbuf[:, ((h % 2) * 2 + sh) * 256:((h % 2) * 2 + sh + 1) * 256],
                          start=(sh == 0), stop=(sh == 1))
            for sh in range(2):
                call = qt.matmul(psS[pb:pb + 64, ysl:ysl + 128], xdwb[:, sl + sh * 64: sl + sh * 64 + 64],
                                 btok[:, (g * 2 + sh) * 128:(g * 2 + sh + 1) * 128],
                                 start=(sh == 0), stop=(sh == 1))
            evh[gh] = s_Y.inc(call, 1)
            if h % 8 == 7 and h // 8 + 2 <= 15:
                rgn[h // 8 + 2] = rows_block(h // 8 + 2)
            if h % 2 == 1:
                qv.wait_ge(s_Y.h, evh[gh])
                ydslot = (hp % 2) * 256
                if ghp >= 2:
                    qv.wait_ge(s_yd[ghp % 2].h, ydst[ghp - 2])
                qv.scalar_tensor_tensor(ydsb[:, ydslot:ydslot + 256], XBCc[:, hp * 256:(hp + 1) * 256],
                                        dbc[:, hp:hp + 1], psY[:, ysl:ysl + 256], ALU.mult, ALU.add)
                if k == 0:
                    qv.tensor_copy(Srun[:, hp * 128:(hp + 1) * 128], psS[:, ysl:ysl + 128])
                else:
                    qv.scalar_tensor_tensor(Srun[:, hp * 128:(hp + 1) * 128], Srun[:, hp * 128:(hp + 1) * 128],
                                            grep[:, hp:hp + 1], psS[:, ysl:ysl + 128], ALU.mult, ALU.add)
                call = qv.tensor_copy(slocbf[:, hp * 128:(hp + 1) * 128], psS[:, ysl:ysl + 128])
                evhp[ghp] = s_ev.inc(call, 1)
                qs.wait_ge(s_ev.h, evhp[ghp])
                ydst[ghp] = s_yd[ghp % 2].inc(qs.dma_start(
                    out=YD.rearrange("(a p) t -> p a t", p=128)[:, hp, t0:t0 + 256],
                    in_=ydsb[:, ydslot:ydslot + 256]), 16)
        if P2A_STAGE >= 3:
            qg.wait_ge(s_ev.h, s_ev.n)
            slp = s_psloc.inc(qg.dma_start(out=SLOC[k * 128:(k + 1) * 128, :], in_=slocbf[:]), 16)
            qv.wait_ge(s_psloc.h, slp)

    qg.wait_ge(s_ev.h, s_ev.n)
    C.chain_dma(qg, s_pend, SST_in, Srun[:])
    qv.tensor_tensor(gtot[:], gam4[:, 0:1], gam4[:, 1:2], ALU.add)
    qv.tensor_tensor(gtot[:], gtot[:], gam4[:, 2:3], ALU.add)
    s_k2v.inc(qv.tensor_tensor(gtot[:], gtot[:], gam4[:, 3:4], ALU.add), 1)
    qg.wait_ge(s_k2v.h, s_k2v.n)
    C.chain_dma(qg, s_pend, GAM_in.rearrange("a h -> h a"), gtot[:])
    C.chain_dma(qg, s_pend, GLOGd, gam4[:])
    qg.wait_ge(s_pend.h, s_pend.n)
    qg.wait_ge(s_pxc.h, s_pxc.n)
    qg.wait_ge(s_psloc.h, s_psloc.n)
    qs.wait_ge(s_yd[0].h, s_yd[0].n)
    qs.wait_ge(s_yd[1].h, s_yd[1].n)
    emit_collective(C, "SST")
    emit_collective(C, "GAM")


def emit_p2b(C, G):
    qs, qg, qv, qc, qt = (C.q[e] for e in ENGINES)
    D = DRAM_SPEC

    ZX = C.dram_t("ZX", *D["ZX"]).ap()
    XCPARK = C.dram_t("XCPARK", *D["XCPARK"]).ap()
    SLOC = C.dram_t("SLOC", *D["SLOC"]).ap()
    YD = C.dram_t("YD", *D["YD"]).ap()
    ACSd = C.dram_t("ACS", *D["ACS"]).ap()
    GLOGd = C.dram_t("GLOG", *D["GLOG"]).ap()
    SST_out = C.dram_t("SST_out", *D["SST_out"]).ap()
    GAM_out = C.dram_t("GAM_out", *D["GAM_out"]).ap()
    YBUF = C.dram_t("YBUF", *D["YBUF"]).ap()
    RMSd = C.dram_t("RMS", *D["RMS"]).ap()
    WMSKT = C.dram_t("WMSKT", *D["WMSKT"]).ap()
    WVALID = C.dram_t("WVALID", *D["WVALID"]).ap()
    SPLIT2 = C.dram_t("SPLIT2", *D["SPLIT2"]).ap()
    PAT2 = C.dram_t("PAT2", *D["PAT2"]).ap()
    PARITY = C.dram_t("PARITY", *D["PARITY"]).ap()
    IDBF = C.dram_t("IDBF", *D["IDBF"]).ap()
    NWC = C.dram_t("nw_cols", *D["nw_cols"]).ap()

    ones_col = G["ones_col"]

    sgath = C.sbuf("b_sgath", [128, DIN], BF16)
    S_in = C.sbuf("b_sin", [128, DIN], F32)
    sinbf = C.sbuf("b_sinbf", [128, DIN], BF16)
    stT = C.sbuf("b_stT", [128, DIN], BF16)
    slbuf = C.sbuf("b_slbuf", [128, DIN], BF16)
    cbuf = C.sbuf("b_cbuf", [128, 8 * 256], BF16)
    zbuf = C.sbuf("b_zbuf", [128, 64 * 256], BF16)
    ydbuf = C.sbuf("b_ydbuf", [128, 64 * 256], BF16)
    rows2 = C.sbuf("b_rows2", [2, 2 * 2048], F32)
    osb = C.sbuf("b_osb", [128, 2 * 256], F32)
    tmp1 = C.sbuf("b_tmp1", [128, 2 * 256], F32)
    tmp2 = C.sbuf("b_tmp2", [128, 2 * 256], F32)
    ygbf = C.sbuf("b_ygbf", [128, 2 * 256], BF16)
    ysq = C.sbuf("b_ysq", [128, 2 * 256], BF16)
    ynbf = C.sbuf("b_ynbf", [128, 2 * 256], BF16)
    gamg = C.sbuf("b_gamg", [4, 128], F32)
    wmsk = C.sbuf("b_wmsk", [4, 4], F32)
    wval = C.sbuf("b_wval", [128, 4], F32)
    wlog = C.sbuf("b_wlog", [128, 4], F32)
    wmat = C.sbuf("b_wmat", [128, 4], F32)
    wrep4 = C.sbuf("b_wrep4", [128, 256], F32)
    split2 = C.sbuf("b_split2", [2, 128], F32)
    pt2 = C.sbuf("b_pt2", [128, 64], F32)
    parity = C.sbuf("b_parity", [128, 128], F32)
    idb = C.sbuf("b_idb", [128, 128], BF16)
    nwc = C.sbuf("b_nwc", [128, 64], F32)
    glog = C.sbuf("b_glog", [128, 4], F32)
    gtot = C.sbuf("b_gtot", [128, 1], F32)
    grep = C.sbuf("b_grep", [128, 64], F32)
    rrow = C.sbuf("b_rrow", [1, TLOC], F32)
    rtmp = C.sbuf("b_rtmp", [1, 256], F32)

    psY = C.psum("b_psY", [128, 1024])
    psD = C.psum("b_psD", [128, 1024])
    psK = C.psum("b_psK", [128, 1024])
    psT = C.psum("b_psT", [128, 1024], BF16)

    s_ld = C.sem("b_ld")
    s_vv = C.sem("b_vv")
    s_cc2 = C.sem("b_cc2")
    s_tt = C.sem("b_tt")
    s_t2 = C.sem("b_t2")
    s_od = C.sem("b_od")
    s_yo = C.sem("b_yo")
    s_ev2 = C.sem("b_ev2")
    s_qq = C.sem("b_qq")
    s_sq2 = C.sem("b_sq2")
    s_yst = [C.sem("b_yst0"), C.sem("b_yst1")]
    s_rg2 = [C.sem("b_rg2a"), C.sem("b_rg2b")]
    s_sg = C.sem("b_sg")
    s_slb = C.sem("b_slb")
    s_ck = C.sem("b_ck")

    ccS = C.sems.get("cc_SST")
    ccG = C.sems.get("cc_GAM")

    for ap_, sb_ in ((WMSKT, wmsk[:]), (WVALID, wval[:]), (SPLIT2, split2[:]),
                     (PAT2, pt2[:]), (PARITY, parity[:]), (IDBF, idb[:]),
                     (NWC, nwc[:]), (GLOGd, glog[:])):
        C.chain_dma(qs, s_ld, sb_, ap_)
    if ccG is not None:
        qs.wait_ge(ccG.h, 1)
    C.chain_dma(qs, s_ld, gamg[:], GAM_out)

    qt.wait_ge(s_ld.h, s_ld.n)
    s_tt.inc(qt.matmul(psK[:, 0:4], gamg[:], wmsk[:], start=True, stop=True), 1)
    qv.wait_ge(s_tt.h, s_tt.n)
    s_vv.inc(qv.tensor_copy(wlog[:], psK[:, 0:4]), 1)
    qc.wait_ge(s_vv.h, s_vv.n)
    s_cc2.inc(qc.activation(wmat[:], wlog[:], AF.Exp), 1)
    qv.wait_ge(s_cc2.h, s_cc2.n)
    s_vv.inc(qv.tensor_tensor(wmat[:], wmat[:], wval[:], ALU.mult), 1)
    qv.wait_ge(s_vv.h, s_vv.n)
    for src in range(4):
        s_vv.inc(qv.tensor_scalar_mul(wrep4[:, src * 64:(src + 1) * 64], pt2[:], wmat[:, src:src + 1]), 1)
    qt.wait_ge(s_vv.h, s_vv.n)
    for src in range(4):
        call = qt.matmul(psK[:, src * 64:(src + 1) * 64], parity[:], wrep4[:, src * 64:(src + 1) * 64],
                         start=True, stop=True)
    s_tt.inc(call, 1)
    qv.wait_ge(s_tt.h, s_tt.n)
    s_vv.inc(qv.tensor_copy(wrep4[:], psK[:, 0:256]), 1)
    qv.wait_ge(s_vv.h, s_vv.n)

    if ccS is not None:
        qs.wait_ge(ccS.h, 1)
    for src in range(4):
        v = s_sg.inc(qs.dma_start(out=sgath[:],
                                  in_=SST_out[src * 128:(src + 1) * 128, :]), 16)
        qv.wait_ge(s_sg.h, v)
        for hp in range(64):
            sl = hp * 128
            if src == 0:
                call = qv.tensor_scalar_mul(S_in[:, hp * 128:(hp + 1) * 128],
                                            sgath[:, sl:sl + 128], wrep4[:, src * 64 + hp:src * 64 + hp + 1])
            else:
                call = qv.scalar_tensor_tensor(S_in[:, hp * 128:(hp + 1) * 128],
                                               sgath[:, sl:sl + 128], wrep4[:, src * 64 + hp:src * 64 + hp + 1],
                                               S_in[:, hp * 128:(hp + 1) * 128], ALU.mult, ALU.add)
        s_vv.inc(call, 1)
        if src < 3:
            qs.wait_ge(s_vv.h, s_vv.n)

    yst = {}
    ev2 = {}
    qq = {}
    for k in range(4):
        t0 = k * 256
        hp0 = k * 64
        if k > 0:
            s_cc2.inc(qc.activation(gtot[:], glog[:, k - 1:k], AF.Exp), 1)
            qc.wait_ge(s_cc2.h, s_cc2.n)
            qv.wait_ge(s_cc2.h, s_cc2.n)
            s_vv.inc(qv.tensor_scalar_mul(grep[:], pt2[:], gtot[:, 0:1]), 1)
            qt.wait_ge(s_vv.h, s_vv.n)
            qt.wait_ge(s_od.h, s_od.n)
            s_tt.inc(qt.matmul(psK[:, 256:320], parity[:], grep[:], start=True, stop=True), 1)
            qs.wait_ge(s_vv.h, s_vv.n)
            v = s_slb.inc(qs.dma_start(out=slbuf[:], in_=SLOC[(k - 1) * 128:k * 128, :]), 16)
            qv.wait_ge(s_tt.h, s_tt.n)
            s_vv.inc(qv.tensor_copy(grep[:], psK[:, 256:320]), 1)
            qv.wait_ge(s_vv.h, s_vv.n)
            qv.wait_ge(s_slb.h, v)
            for hp in range(64):
                call = qv.scalar_tensor_tensor(S_in[:, hp * 128:(hp + 1) * 128],
                                               S_in[:, hp * 128:(hp + 1) * 128], grep[:, hp:hp + 1],
                                               slbuf[:, hp * 128:(hp + 1) * 128], ALU.mult, ALU.add)
            s_vv.inc(call, 1)
        s_vv.inc(qv.tensor_copy(sinbf[:], S_in[:]), 1)
        if k == 0:
            DBGd = C.dram_t("DBG", *D["DBG"]).ap()
            s_dbg = C.sem("b_dbg")
            qg.wait_ge(s_vv.h, s_vv.n)
            C.chain_dma(qg, s_dbg, DBGd[0:128, :], S_in[:])
            C.chain_dma(qg, s_dbg, DBGd[128:129, 0:4], wmat[0:1, :])
            C.chain_dma(qg, s_dbg, DBGd[129:130, 0:256], wrep4[0:1, :])
            qg.wait_ge(s_dbg.h, s_dbg.n)
        qt.wait_ge(s_vv.h, s_vv.n)
        for grp8 in range(8):
            for i8 in range(8):
                hp = grp8 * 8 + i8
                call = qt.transpose(psT[:, i8 * 128:(i8 + 1) * 128], sinbf[:, hp * 128:(hp + 1) * 128], idb[:])
            s_tt.inc(call, 1)
            qv.wait_ge(s_tt.h, s_tt.n)
            s_vv.inc(qv.tensor_copy(stT[:, grp8 * 1024:(grp8 + 1) * 1024], psT[:, 0:1024]), 1)
            qt.wait_ge(s_vv.h, s_vv.n)
        if k > 0:
            qs.wait_ge(s_ev2.h, s_ev2.n)
            qs.wait_ge(s_t2.h, s_t2.n)
        def rows2_block(b2, k=k, t0=t0):
            slot = (b2 % 2) * 2048
            sr = s_rg2[b2 % 2]
            if 8 * k + b2 >= 2:
                qs.wait_ge(s_t2.h, 64 * k + 8 * (b2 - 1))
            sr.inc(qs.dma_start(out=rows2[0:1, slot:slot + 2048].rearrange("a (p t) -> a p t", t=256),
                                in_=ACSd[8 * b2:8 * b2 + 8, t0:t0 + 256]), 16)
            sr.inc(qs.dma_start(out=rows2[1:2, slot:slot + 2048].rearrange("a (p t) -> a p t", t=256),
                                in_=ACSd[64 + 8 * b2:64 + 8 * b2 + 8, t0:t0 + 256]), 16)
            return sr.n

        rg2n = {}
        rg2n[0] = rows2_block(0)
        rg2n[1] = rows2_block(1)
        C.chain_dma(qs, s_ck, cbuf[:].rearrange("p (a t) -> p a t", t=256),
                    XCPARK.rearrange("(a p) t -> p a t", p=128)[:, 72:80, t0:t0 + 256])
        C.chain_dma(qs, s_ck, zbuf[:].rearrange("p (a t) -> p a t", t=256),
                    ZX.rearrange("(a p) t -> p a t", p=128)[:, 0:64, t0:t0 + 256])
        C.chain_dma(qs, s_ck, ydbuf[:].rearrange("p (a t) -> p a t", t=256),
                    YD.rearrange("(a p) t -> p a t", p=128)[:, :, t0:t0 + 256])
        ldk = s_ck.n
        qc.wait_ge(s_ck.h, ldk)
        qc.wait_ge(s_ev2.h, s_ev2.n)
        s_cc2.inc(qc.activation(zbuf[:], zbuf[:], AF.Silu), 1)
        silk = s_cc2.n
        base_tt = s_tt.n

        for hp in range(64):
            ghp = hp0 + hp
            sl2 = (hp % 2)
            qt.wait_ge(s_ck.h, ldk)
            b2, r2 = hp // 8, hp % 8
            if ghp >= 2:
                qt.wait_ge(s_od.h, ghp - 1)
            qt.wait_ge(s_rg2[b2 % 2].h, rg2n[b2])
            call = qt.matmul(psD[:, sl2 * 512:sl2 * 512 + 256], split2[:],
                             rows2[:, (b2 % 2) * 2048 + r2 * 256:(b2 % 2) * 2048 + (r2 + 1) * 256],
                             start=True, stop=True)
            s_t2.inc(call, 1)
            qc.wait_ge(s_t2.h, s_t2.n)
            if ghp >= 2:
                qc.wait_ge(s_ev2.h, ev2[ghp - 2])
            call = qc.activation(osb[:, sl2 * 256:(sl2 + 1) * 256], psD[:, sl2 * 512:sl2 * 512 + 256], AF.Exp)
            s_od.inc(call, 1)
            if ghp >= 2:
                qt.wait_ge(s_ev2.h, ev2[ghp - 2])
            g = (2 * hp) // 16
            for par in range(2):
                call = qt.matmul(psY[par * 64:par * 64 + 64, sl2 * 512:sl2 * 512 + 256],
                                 stT[:, hp * 128 + par * 64: hp * 128 + par * 64 + 64],
                                 cbuf[:, g * 256:(g + 1) * 256], start=True, stop=True)
            s_yo.inc(call, 1)
            qv.wait_ge(s_od.h, s_od.n)
            qv.wait_ge(s_yo.h, s_yo.n)
            qv.wait_ge(s_cc2.h, silk)
            if ghp >= 2:
                qv.wait_ge(s_qq.h, qq[ghp - 2])
                qv.wait_ge(s_yst[ghp % 2].h, yst[ghp - 2])
            qv.tensor_tensor(tmp1[:, sl2 * 256:(sl2 + 1) * 256], psY[:, sl2 * 512:sl2 * 512 + 256],
                             osb[:, sl2 * 256:(sl2 + 1) * 256], ALU.mult)
            qv.tensor_tensor(tmp2[:, sl2 * 256:(sl2 + 1) * 256], tmp1[:, sl2 * 256:(sl2 + 1) * 256],
                             ydbuf[:, hp * 256:(hp + 1) * 256], ALU.add)
            qv.tensor_tensor(ygbf[:, sl2 * 256:(sl2 + 1) * 256], tmp2[:, sl2 * 256:(sl2 + 1) * 256],
                             zbuf[:, hp * 256:(hp + 1) * 256], ALU.mult)
            call = qv.tensor_scalar_mul(ynbf[:, sl2 * 256:(sl2 + 1) * 256],
                                        ygbf[:, sl2 * 256:(sl2 + 1) * 256], nwc[:, hp:hp + 1])
            ev2[ghp] = s_ev2.inc(call, 1)
            qc.wait_ge(s_ev2.h, ev2[ghp])
            if ghp >= 2:
                qc.wait_ge(s_qq.h, qq[ghp - 2])
            sq2 = s_sq2.inc(qc.activation(ysq[:, sl2 * 256:(sl2 + 1) * 256],
                                          ygbf[:, sl2 * 256:(sl2 + 1) * 256], AF.Square), 1)
            qt.wait_ge(s_sq2.h, sq2)
            call = qt.matmul(psK[0:1, 512:768],
                             ones_col[:], ysq[:, sl2 * 256:(sl2 + 1) * 256],
                             start=(hp == 0), stop=(hp == 63))
            qq[ghp] = s_qq.inc(call, 1)
            qs.wait_ge(s_ev2.h, ev2[ghp])
            yst[ghp] = s_yst[ghp % 2].inc(qs.dma_start(
                out=YBUF.rearrange("(a p) t -> p a t", p=128)[:, hp, t0:t0 + 256],
                in_=ynbf[:, sl2 * 256:(sl2 + 1) * 256]), 16)
            if r2 == 7 and b2 + 2 <= 7:
                rg2n[b2 + 2] = rows2_block(b2 + 2)
        qv.wait_ge(s_qq.h, s_qq.n)
        s_vv.inc(qv.tensor_scalar(rtmp[:], psK[0:1, 512:768], 1.0 / DIN, EPS, ALU.mult, ALU.add), 1)
        qc.wait_ge(s_vv.h, s_vv.n)
        s_cc2.inc(qc.activation(rtmp[:], rtmp[:], AF.Ln), 1)
        qc.wait_ge(s_cc2.h, s_cc2.n)
        s_cc2.inc(qc.activation(rrow[0:1, t0:t0 + 256], rtmp[:], AF.Exp, scale=-0.5), 1)
        qt.wait_ge(s_vv.h, s_vv.n)
    qs.wait_ge(s_cc2.h, s_cc2.n)
    s_rms = C.sem("b_rms")
    s_rms.inc(qs.dma_start(out=RMSd, in_=rrow[:]), 16)
    qs.wait_ge(s_rms.h, s_rms.n)
    qs.wait_ge(s_yst[0].h, s_yst[0].n)
    qs.wait_ge(s_yst[1].h, s_yst[1].n)


def emit_p3(C, G):
    qs, qg, qv, qc, qt = (C.q[e] for e in ENGINES)
    D = DRAM_SPEC
    YBUF = C.dram_t("YBUF", *D["YBUF"]).ap()
    RMSd = C.dram_t("RMS", *D["RMS"]).ap()
    WOT = C.dram_t("wot", *D["wot"]).ap()
    x0 = C.dram_t("x0", *D["x0"]).ap()
    XRES = C.dram_t("XRES", *D["XRES"]).ap()
    ones_row_f = G["ones_row_f"]

    ABR = C.dram_t("LN0AB", *D["LN0AB"]).ap()
    wsb = C.sbuf("p3_w", [128, 2 * 512], BF16)
    ysb = C.sbuf("p3_y", [128, 2 * TLOC], BF16)
    x0sb = C.sbuf("p3_x0", [128, 4 * TLOC], F32)
    x1sb = C.sbuf("p3_x1", [128, 4 * TLOC], F32)
    rrow = C.sbuf("p3_rrow", [1, TLOC], F32)
    rbc = C.sbuf("p3_rbc", [128, TLOC], F32)
    abrow = C.sbuf("p3_abrow", [1, 2 * TLOC], F32)
    abx = C.sbuf("p3_abx", [128, 2 * TLOC], F32)
    scr = C.sbuf("p3_scr", [128, TLOC], F32)
    psM = C.psum("p3_psM", [128, 4 * TLOC])

    s_w = [C.sem("p3_w0"), C.sem("p3_w1")]
    s_y = [C.sem("p3_y0"), C.sem("p3_y1")]
    s_x = C.sem("p3_x")
    s_r = C.sem("p3_r")
    s_pe = C.sem("p3_pe")
    s_ev = C.sem("p3_ev")
    s_st = [C.sem("p3_st0"), C.sem("p3_st1")]

    v = s_r.inc(qs.dma_start(out=rrow[:], in_=RMSd), 16)
    v = s_r.inc(qs.dma_start(out=abrow[:], in_=ABR), 16)
    qt.wait_ge(s_r.h, v)
    for th in range(2):
        qt.matmul(psM[:, 3 * TLOC + th * 512: 3 * TLOC + (th + 1) * 512],
                  ones_row_f[:], rrow[0:1, th * 512:(th + 1) * 512], start=True, stop=True)
    for th in range(2):
        call = qt.matmul(psM[:, 2 * TLOC + th * 512: 2 * TLOC + (th + 1) * 512],
                         ones_row_f[:], abrow[0:1, th * 512:(th + 1) * 512], start=True, stop=True)
    tt0 = s_pe.inc(call, 1)
    qv.wait_ge(s_pe.h, tt0)
    qv.tensor_copy(rbc[:], psM[:, 3 * TLOC:4 * TLOC])
    ev0a = s_ev.inc(qv.tensor_copy(abx[:, 0:TLOC], psM[:, 2 * TLOC:3 * TLOC]), 1)
    qt.wait_ge(s_ev.h, ev0a)
    for th in range(2):
        call = qt.matmul(psM[:, 2 * TLOC + th * 512: 2 * TLOC + (th + 1) * 512],
                         ones_row_f[:], abrow[0:1, TLOC + th * 512: TLOC + (th + 1) * 512],
                         start=True, stop=True)
    pe0 = s_pe.inc(call, 1)
    qv.wait_ge(s_pe.h, pe0)
    ev0 = s_ev.inc(qv.tensor_copy(abx[:, TLOC:2 * TLOC], psM[:, 2 * TLOC:3 * TLOC]), 1)

    stn = {}
    for dmg in range(8):
        qs.wait_ge(s_ev.h, s_ev.n)
        for i4 in range(4):
            C.chain_dma(qs, s_x, x0sb[:, i4 * TLOC:(i4 + 1) * TLOC],
                        x0[(dmg * 4 + i4) * 128:(dmg * 4 + i4 + 1) * 128, :])
        xldn = s_x.n
        for fc in range(64):
            it = dmg * 64 + fc
            sl = (it % 2)
            if it >= 2:
                qg.wait_ge(s_pe.h, pe0 + it - 1)
            wv = s_w[sl].inc(qg.dma_start(out=wsb[:, sl * 512:(sl + 1) * 512], in_=WOT[dmg, fc]), 16)
            if it >= 2:
                qs.wait_ge(s_pe.h, pe0 + it - 1)
            yv = s_y[sl].inc(qs.dma_start(out=ysb[:, sl * TLOC:(sl + 1) * TLOC],
                                          in_=YBUF[fc * 128:(fc + 1) * 128, :]), 16)
            qt.wait_ge(s_w[sl].h, wv)
            qt.wait_ge(s_y[sl].h, yv)
            if fc == 0:
                qt.wait_ge(s_ev.h, s_ev.n)
            for dmc in range(4):
                for th in range(2):
                    call = qt.matmul(psM[:, dmc * TLOC + th * 512: dmc * TLOC + (th + 1) * 512],
                                     wsb[:, sl * 512 + dmc * 128: sl * 512 + (dmc + 1) * 128],
                                     ysb[:, sl * TLOC + th * 512: sl * TLOC + (th + 1) * 512],
                                     start=(fc == 0), stop=(fc == 63))
            s_pe.inc(call, 1)
        qv.wait_ge(s_pe.h, s_pe.n)
        qv.wait_ge(s_x.h, xldn)
        for dmc in range(4):
            if dmg >= 1:
                qv.wait_ge(s_st[(dmg - 1) % 2].h, stn[(dmg - 1, dmc)])
            qv.tensor_tensor(scr[:], x0sb[:, dmc * TLOC:(dmc + 1) * TLOC], abx[:, 0:TLOC], ALU.mult)
            qv.tensor_tensor(scr[:], scr[:], abx[:, TLOC:2 * TLOC], ALU.add)
            qv.tensor_tensor(x1sb[:, dmc * TLOC:(dmc + 1) * TLOC], psM[:, dmc * TLOC:(dmc + 1) * TLOC],
                             rbc[:], ALU.mult)
            qv.tensor_tensor(x1sb[:, dmc * TLOC:(dmc + 1) * TLOC],
                             x1sb[:, dmc * TLOC:(dmc + 1) * TLOC],
                             x0sb[:, dmc * TLOC:(dmc + 1) * TLOC], ALU.add)
            call = qv.tensor_tensor(x1sb[:, dmc * TLOC:(dmc + 1) * TLOC],
                                    x1sb[:, dmc * TLOC:(dmc + 1) * TLOC],
                                    scr[:], ALU.add)
            ev = s_ev.inc(call, 1)
            qs.wait_ge(s_ev.h, ev)
            stn[(dmg, dmc)] = s_st[dmg % 2].inc(qs.dma_start(
                out=XRES[(dmg * 4 + dmc) * 128:(dmg * 4 + dmc + 1) * 128, :],
                in_=x1sb[:, dmc * TLOC:(dmc + 1) * TLOC]), 16)
    qs.wait_ge(s_st[0].h, s_st[0].n)
    qs.wait_ge(s_st[1].h, s_st[1].n)


def emit_ln(C, G, src, XBF, psM, tag):
    qs, qg, qv, qc, qt = (C.q[e] for e in ENGINES)
    ones_col = G["ones_col"]
    ones_row_f = G["ones_row_f"]
    xsq = C.sbuf(tag + "_xsq", [128, 2 * TLOC], BF16)
    stat = C.sbuf(tag + "_stat", [1, 3 * TLOC], F32)
    ab = C.sbuf(tag + "_ab", [128, 2 * TLOC], F32)
    if not hasattr(C, "sbufs_dbg"):
        C.sbufs_dbg = {}
    C.sbufs_dbg[tag + "_stat"] = stat
    C.sbufs_dbg[tag + "_ab"] = ab
    s_xld = [C.sem(tag + "_xld0"), C.sem(tag + "_xld1")]
    s_sq = C.sem(tag + "_sq")
    s_stmm = C.sem(tag + "_stmm")
    s_ab = C.sem(tag + "_ab")
    s_ln = C.sem(tag + "_ln")
    psS = psM[0:1, 0:2 * TLOC]
    for c in range(32):
        sl = (c % 2) * TLOC
        v = s_xld[c % 2].inc(qg.dma_start(out=XBF[:, c * TLOC:(c + 1) * TLOC],
                                          in_=src[c * 128:(c + 1) * 128, :]), 16)
        qc.wait_ge(s_xld[c % 2].h, v)
        if c >= 2:
            qc.wait_ge(s_stmm.h, c - 1)
        s_sq.inc(qc.activation(xsq[:, sl:sl + TLOC], XBF[:, c * TLOC:(c + 1) * TLOC], AF.Square), 1)
        qt.wait_ge(s_xld[c % 2].h, v)
        qt.wait_ge(s_sq.h, c + 1)
        first = (c == 0)
        for th in range(2):
            qt.matmul(psS[0:1, th * 512:(th + 1) * 512],
                      ones_col[:], XBF[:, c * TLOC + th * 512: c * TLOC + (th + 1) * 512],
                      start=first, stop=(c == 31))
        for th in range(2):
            call = qt.matmul(psS[0:1, TLOC + th * 512: TLOC + (th + 1) * 512],
                             ones_col[:], xsq[:, sl + th * 512: sl + (th + 1) * 512],
                             start=first, stop=(c == 31))
        s_stmm.inc(call, 1)
    qv.wait_ge(s_stmm.h, 32)
    mrow = stat[0:1, 0:TLOC]
    vrow = stat[0:1, TLOC:2 * TLOC]
    trow = stat[0:1, 2 * TLOC:3 * TLOC]
    qv.tensor_scalar_mul(mrow, psS[0:1, 0:TLOC], 1.0 / DM)
    qv.tensor_scalar_mul(vrow, psS[0:1, TLOC:2 * TLOC], 1.0 / DM)
    qv.tensor_tensor(trow, mrow, mrow, ALU.mult)
    s_ab.inc(qv.tensor_tensor(vrow, vrow, trow, ALU.subtract), 1)
    qc.wait_ge(s_ab.h, s_ab.n)
    s_ab.inc(qc.activation(trow, vrow, AF.Ln, bias=EPS), 1)
    qc.wait_ge(s_ab.h, s_ab.n)
    s_ab.inc(qc.activation(vrow, trow, AF.Exp, scale=-0.5), 1)
    qv.wait_ge(s_ab.h, s_ab.n)
    qv.tensor_tensor(trow, mrow, vrow, ALU.mult)
    s_ab.inc(qv.tensor_scalar_mul(trow, trow, -1.0), 1)
    qt.wait_ge(s_ab.h, s_ab.n)
    for th in range(2):
        qt.matmul(psM[:, 2 * TLOC + th * 512: 2 * TLOC + (th + 1) * 512],
                  ones_row_f[:], vrow[0:1, th * 512:(th + 1) * 512], start=True, stop=True)
    for th in range(2):
        call = qt.matmul(psM[:, 3 * TLOC + th * 512: 3 * TLOC + (th + 1) * 512],
                         ones_row_f[:], trow[0:1, th * 512:(th + 1) * 512], start=True, stop=True)
    s_ab.inc(call, 1)
    qv.wait_ge(s_ab.h, s_ab.n)
    qv.tensor_copy(ab[:, 0:TLOC], psM[:, 2 * TLOC:3 * TLOC])
    s_ab.inc(qv.tensor_copy(ab[:, TLOC:2 * TLOC], psM[:, 3 * TLOC:4 * TLOC]), 1)
    tmp32 = C.sbuf(tag + "_t32", [128, 2 * TLOC], F32)
    for c in range(32):
        sl = (c % 2) * TLOC
        qv.tensor_tensor(tmp32[:, sl:sl + TLOC], XBF[:, c * TLOC:(c + 1) * TLOC], ab[:, 0:TLOC], ALU.mult)
        call = qv.tensor_tensor(XBF[:, c * TLOC:(c + 1) * TLOC], tmp32[:, sl:sl + TLOC], ab[:, TLOC:2 * TLOC], ALU.add)
    s_ln.inc(call, 1)
    qt.wait_ge(s_ln.h, 1)
    qc.wait_ge(s_ln.h, 1)
    qs.wait_ge(s_ln.h, 1)
    qg.wait_ge(s_ln.h, 1)
    return s_ln


def emit_p4(C, G):
    qs, qg, qv, qc, qt = (C.q[e] for e in ENGINES)
    D = DRAM_SPEC
    XRES = C.dram_t("XRES", *D["XRES"]).ap()
    XRES2 = C.dram_t("XRES2", *D["XRES2"]).ap()
    RT = C.dram_t("routerT", *D["routerT"]).ap()
    AT = C.dram_t("AT", *D["AT"]).ap()
    WB = C.dram_t("wbaseT", *D["wbaseT"]).ap()
    BT = C.dram_t("BT4", *D["BT4"]).ap()
    BB = C.dram_t("baseb", *D["baseb"]).ap()
    IDF32 = C.dram_t("IDF32", *D["IDF32"]).ap()
    CROWD = C.dram_t("CROW", *D["CROW"]).ap()
    ones_row_f = G["ones_row_f"]

    XBF = C.sbuf("p4_xbf", [128, 32 * TLOC], BF16)
    psM = C.psum("p4_psM", [128, 4 * TLOC])
    emit_ln(C, G, XRES, XBF, psM, "p4ln")
    rtsb = C.sbuf("p4_rt", [128, 32 * 16], BF16)
    atsb = C.sbuf("p4_at", [128, 32 * 64], BF16)
    idf = C.sbuf("p4_idf", [128, 128], F32)
    lg = C.sbuf("p4_lg", [16, TLOC], F32)
    ltok = C.sbuf("p4_ltok", [128, 8 * 16], F32)
    ctok = C.sbuf("p4_ctok", [128, 8 * 16], F32)
    cfm = C.sbuf("p4_cfm", [16, TLOC], F32)
    crow = C.sbuf("p4_crow", [1, 2 * TLOC], BF16)
    resbf = C.sbuf("p4_res", [64, TLOC], BF16)
    sres = C.sbuf("p4_sres", [64, 16 * TLOC], BF16)
    scol = C.sbuf("p4_scol", [128, 8], F32)
    wsb = C.sbuf("p4_w", [128, 2 * 4096], BF16)
    btsb = C.sbuf("p4_bt", [64, 2 * 2048], BF16)
    bbsb = C.sbuf("p4_bb", [128, 32], F32)
    x1sb = C.sbuf("p4_x1", [128, 2 * TLOC], F32)
    x2sb = C.sbuf("p4_x2", [128, 2 * TLOC], F32)
    xsq_scr = C.sbuf("p4_scr", [128, 32], F32)
    s_ld = C.sem("p4_ld")
    s_tt = C.sem("p4_tt")
    s_vv = C.sem("p4_vv")
    s_cc = C.sem("p4_cc")
    s_crw = C.sem("p4_crw")
    s_w = [C.sem("p4_w0"), C.sem("p4_w1")]
    s_bt = [C.sem("p4_bt0"), C.sem("p4_bt1")]
    s_x1 = [C.sem("p4_x10"), C.sem("p4_x11")]
    s_pe = C.sem("p4_pe")
    s_ev = C.sem("p4_ev")
    s_st = [C.sem("p4_st0"), C.sem("p4_st1")]

    s_ldg = C.sem("p4_ldg")
    C.chain_dma(qg, s_ldg, rtsb[:].rearrange("p (a b) -> p a b", b=16), RT.rearrange("a p b -> p a b"))
    C.chain_dma(qg, s_ldg, atsb[:].rearrange("p (a b) -> p a b", b=64), AT.rearrange("a p b -> p a b"))
    C.chain_dma(qs, s_ld, idf[:], IDF32)
    C.chain_dma(qs, s_ld, bbsb[:], BB)
    qt.wait_ge(s_ld.h, s_ld.n)
    qt.wait_ge(s_ldg.h, s_ldg.n)

    for dc in range(32):
        for th in range(2):
            qt.matmul(psM[0:16, th * 512:(th + 1) * 512], rtsb[:, dc * 16:(dc + 1) * 16],
                      XBF[:, dc * TLOC + th * 512: dc * TLOC + (th + 1) * 512],
                      start=(dc == 0), stop=(dc == 31))
        for th in range(2):
            call = qt.matmul(psM[0:64, TLOC + th * 512: TLOC + (th + 1) * 512],
                             atsb[:, dc * 64:(dc + 1) * 64],
                             XBF[:, dc * TLOC + th * 512: dc * TLOC + (th + 1) * 512],
                             start=(dc == 0), stop=(dc == 31))
    s_tt.inc(call, 1)
    qv.wait_ge(s_tt.h, s_tt.n)
    s_vv.inc(qv.tensor_copy(lg[:], psM[0:16, 0:TLOC]), 1)
    s_vv.inc(qv.tensor_copy(resbf[:], psM[0:64, TLOC:2 * TLOC]), 1)
    qt.wait_ge(s_vv.h, s_vv.n)
    for c8 in range(8):
        call = qt.transpose(psM[:, 2 * TLOC + c8 * 16:2 * TLOC + (c8 + 1) * 16],
                            lg[0:16, c8 * 128:(c8 + 1) * 128], idf[0:16, 0:16])
    s_tt.inc(call, 1)
    qv.wait_ge(s_tt.h, s_tt.n)
    s_vv.inc(qv.tensor_copy(ltok[:], psM[:, 2 * TLOC:2 * TLOC + 128]), 1)
    qv.wait_ge(s_vv.h, s_vv.n)
    m1 = scol[:, 0:1]
    m2 = scol[:, 1:2]
    negm = scol[:, 2:3]
    ssum = scol[:, 3:4]
    rec = scol[:, 4:5]
    w1 = scol[:, 5:6]
    w2 = scol[:, 6:7]
    den = scol[:, 7:8]
    for c8 in range(8):
        L = ltok[:, c8 * 16:(c8 + 1) * 16]
        Ck = ctok[:, c8 * 16:(c8 + 1) * 16]
        s_vv.inc(qv.reduce_max(m1, L, AX.X), 1)
        qv.wait_ge(s_vv.h, s_vv.n)
        s_vv.inc(qv.tensor_scalar_mul(negm, m1, -1.0), 1)
        qv.wait_ge(s_vv.h, s_vv.n)
        qc.wait_ge(s_vv.h, s_vv.n)
        if c8 > 0:
            qc.wait_ge(s_vv.h, s_vv.n)
        s_cc.inc(qc.activation(Ck, L, AF.Exp, bias=negm[:, 0:1]), 1)
        qv.wait_ge(s_cc.h, s_cc.n)
        s_vv.inc(qv.reduce_sum(ssum, Ck, AX.X), 1)
        qv.wait_ge(s_vv.h, s_vv.n)
        s_vv.inc(qv.reciprocal(rec, ssum), 1)
        qv.wait_ge(s_vv.h, s_vv.n)
        s_vv.inc(qv.tensor_scalar_mul(Ck, Ck, rec[:, 0:1]), 1)
        qv.wait_ge(s_vv.h, s_vv.n)
        s_vv.inc(qv.reduce_max(m1, Ck, AX.X), 1)
        qv.wait_ge(s_vv.h, s_vv.n)
        eq = ltok[:, c8 * 16:(c8 + 1) * 16]
        s_vv.inc(qv.tensor_single_scalar(eq, Ck, m1[:, 0:1], ALU.is_equal), 1)
        qv.wait_ge(s_vv.h, s_vv.n)
        s_vv.inc(qv.tensor_tensor(xsq_scr[:, 0:16], Ck, eq, ALU.mult), 1)
        qv.wait_ge(s_vv.h, s_vv.n)
        s_vv.inc(qv.tensor_tensor(xsq_scr[:, 16:32], Ck, xsq_scr[:, 0:16], ALU.subtract), 1)
        qv.wait_ge(s_vv.h, s_vv.n)
        s_vv.inc(qv.reduce_max(m2, xsq_scr[:, 16:32], AX.X), 1)
        qv.wait_ge(s_vv.h, s_vv.n)
        s_vv.inc(qv.tensor_tensor(den, m1, m2, ALU.add), 1)
        qv.wait_ge(s_vv.h, s_vv.n)
        s_vv.inc(qv.tensor_scalar_add(den, den, 1e-6), 1)
        qv.wait_ge(s_vv.h, s_vv.n)
        s_vv.inc(qv.reciprocal(rec, den), 1)
        qv.wait_ge(s_vv.h, s_vv.n)
        s_vv.inc(qv.tensor_tensor(w1, m1, rec, ALU.mult), 1)
        s_vv.inc(qv.tensor_tensor(w2, m2, rec, ALU.mult), 1)
        qv.wait_ge(s_vv.h, s_vv.n)
        s_vv.inc(qv.tensor_scalar_mul(w1, w1, 0.5), 1)
        s_vv.inc(qv.tensor_scalar_mul(w2, w2, 0.5), 1)
        qv.wait_ge(s_vv.h, s_vv.n)
        s_vv.inc(qv.tensor_single_scalar(xsq_scr[:, 0:16], xsq_scr[:, 16:32], m2[:, 0:1], ALU.is_equal), 1)
        qv.wait_ge(s_vv.h, s_vv.n)
        s_vv.inc(qv.tensor_scalar_mul(Ck, eq, w1[:, 0:1]), 1)
        qv.wait_ge(s_vv.h, s_vv.n)
        s_vv.inc(qv.scalar_tensor_tensor(Ck, xsq_scr[:, 0:16], w2[:, 0:1], Ck, ALU.mult, ALU.add), 1)
        qv.wait_ge(s_vv.h, s_vv.n)
    qt.wait_ge(s_vv.h, s_vv.n)
    for c8 in range(8):
        call = qt.transpose(psM[0:16, 2 * TLOC + 256 + c8 * 128:2 * TLOC + 256 + (c8 + 1) * 128],
                            ctok[:, c8 * 16:(c8 + 1) * 16], idf[:])
    s_tt.inc(call, 1)
    qv.wait_ge(s_tt.h, s_tt.n)
    s_vv.inc(qv.tensor_copy(cfm[:], psM[0:16, 2 * TLOC + 256:2 * TLOC + 256 + TLOC]), 1)
    qg.wait_ge(s_vv.h, s_vv.n)
    s_crw.inc(qg.dma_start(out=CROWD, in_=cfm[:]), 16)
    qs.wait_ge(s_crw.h, s_crw.n)
    s_cr2 = [C.sem("p4_cr20"), C.sem("p4_cr21")]
    crn = {}
    for e in (0, 1):
        crn[e] = s_cr2[e % 2].inc(qs.dma_start(out=crow[0:1, (e % 2) * TLOC:(e % 2 + 1) * TLOC],
                                               in_=CROWD[e:e + 1, :]), 16)
    for e in range(16):
        qt.wait_ge(s_cr2[e % 2].h, crn[e])
        for th in range(2):
            call = qt.matmul(psM[:, 3 * TLOC + th * 512:3 * TLOC + (th + 1) * 512],
                             G["ones_row_b"][:], crow[0:1, (e % 2) * TLOC + th * 512: (e % 2) * TLOC + (th + 1) * 512],
                             start=True, stop=True)
        s_tt.inc(call, 1)
        if e + 2 < 16:
            qs.wait_ge(s_tt.h, s_tt.n)
            crn[e + 2] = s_cr2[e % 2].inc(qs.dma_start(out=crow[0:1, (e % 2) * TLOC:(e % 2 + 1) * TLOC],
                                                       in_=CROWD[e + 2:e + 2 + 1, :]), 16)
        qv.wait_ge(s_tt.h, s_tt.n)
        for th in range(2):
            call = qv.tensor_tensor(sres[:, e * TLOC + th * 512:e * TLOC + (th + 1) * 512],
                                    resbf[:, th * 512:(th + 1) * 512],
                                    psM[0:64, 3 * TLOC + th * 512:3 * TLOC + (th + 1) * 512], ALU.mult)
        s_vv.inc(call, 1)
        qt.wait_ge(s_vv.h, s_vv.n)
    base_pe = s_pe.n
    base_ev = s_ev.n
    stn = {}
    evn = {}
    for dmc in range(32):
        sl = dmc % 2
        if dmc >= 2:
            qg.wait_ge(s_pe.h, base_pe + dmc - 1)
        wv = s_w[sl].inc(qg.dma_start(out=wsb[:, sl * 4096:(sl + 1) * 4096], in_=WB[dmc]), 16)
        btv = s_bt[sl].inc(qg.dma_start(out=btsb[:, sl * 2048:(sl + 1) * 2048], in_=BT[dmc]), 16)
        if dmc >= 2:
            qs.wait_ge(s_ev.h, evn[dmc - 2])
        x1v = s_x1[sl].inc(qs.dma_start(out=x1sb[:, sl * TLOC:(sl + 1) * TLOC],
                                        in_=XRES[dmc * 128:(dmc + 1) * 128, :]), 16)
        qt.wait_ge(s_w[sl].h, wv)
        qt.wait_ge(s_bt[sl].h, btv)
        if dmc >= 2:
            qt.wait_ge(s_ev.h, evn[dmc - 2])
        psl = sl * 2 * 512
        for dc in range(32):
            for th in range(2):
                qt.matmul(psM[:, psl + th * 512: psl + (th + 1) * 512],
                          wsb[:, sl * 4096 + dc * 128: sl * 4096 + (dc + 1) * 128],
                          XBF[:, dc * TLOC + th * 512: dc * TLOC + (th + 1) * 512],
                          start=(dc == 0), stop=False)
        for e in range(16):
            for th in range(2):
                call = qt.matmul(psM[:, psl + th * 512: psl + (th + 1) * 512],
                                 btsb[:, sl * 2048 + e * 128: sl * 2048 + (e + 1) * 128],
                                 sres[:, e * TLOC + th * 512: e * TLOC + (th + 1) * 512],
                                 start=False, stop=(e == 15))
        s_pe.inc(call, 1)
        qv.wait_ge(s_pe.h, s_pe.n)
        qv.wait_ge(s_x1[sl].h, x1v)
        if dmc >= 2:
            qv.wait_ge(s_st[sl].h, stn[dmc - 2])
        for th in range(2):
            qv.tensor_scalar_add(x2sb[:, sl * TLOC + th * 512:sl * TLOC + (th + 1) * 512],
                                 psM[:, psl + th * 512: psl + (th + 1) * 512], bbsb[:, dmc:dmc + 1])
            call = qv.tensor_tensor(x2sb[:, sl * TLOC + th * 512:sl * TLOC + (th + 1) * 512],
                                    x2sb[:, sl * TLOC + th * 512:sl * TLOC + (th + 1) * 512],
                                    x1sb[:, sl * TLOC + th * 512:sl * TLOC + (th + 1) * 512], ALU.add)
        evn[dmc] = s_ev.inc(call, 1)
        qs.wait_ge(s_ev.h, evn[dmc])
        stn[dmc] = s_st[sl].inc(qs.dma_start(out=XRES2[dmc * 128:(dmc + 1) * 128, :],
                                             in_=x2sb[:, sl * TLOC:(sl + 1) * TLOC]), 16)
    qs.wait_ge(s_st[0].h, s_st[0].n)
    qs.wait_ge(s_st[1].h, s_st[1].n)


def emit_p5(C, G):
    qs, qg, qv, qc, qt = (C.q[e] for e in ENGINES)
    D = DRAM_SPEC
    XRES2 = C.dram_t("XRES2", *D["XRES2"]).ap()
    WQKV = C.dram_t("wqkvT", *D["wqkvT"]).ap()
    QROPE = C.dram_t("QROPE", *D["QROPE"]).ap()
    KV_in = C.dram_t("KV_in", *D["KV_in"]).ap()
    JROT = C.dram_t("JROT", *D["JROT"]).ap()
    RCOS = C.dram_t("RCOS", *D["RCOS"]).ap()
    RSIN = C.dram_t("RSIN", *D["RSIN"]).ap()

    XBF = C.sbuf("p5_xbf", [128, 32 * TLOC], BF16)
    psM = C.psum("p5_psM", [128, 4 * TLOC])
    emit_ln(C, G, XRES2, XBF, psM, "p5ln")

    wsb = C.sbuf("p5_w", [128, 2 * 4096], BF16)
    qtile = C.sbuf("p5_qt", [128, 2 * TLOC], BF16)
    rop1 = C.sbuf("p5_r1", [128, TLOC], F32)
    rop2 = C.sbuf("p5_r2", [128, TLOC], F32)
    rout = C.sbuf("p5_ro", [128, 2 * TLOC], BF16)
    jrot = C.sbuf("p5_j", [128, 128], BF16)
    rcos = C.sbuf("p5_cos", [128, TLOC], F32)
    rsin = C.sbuf("p5_sin", [128, TLOC], F32)

    s_ld = C.sem("p5_ld")
    s_w = [C.sem("p5_w0"), C.sem("p5_w1")]
    s_pe = C.sem("p5_pe")
    s_pj = C.sem("p5_pj")
    s_ev = C.sem("p5_ev")
    s_st = [C.sem("p5_st0"), C.sem("p5_st1")]

    for ap_, sb_ in ((JROT, jrot[:]), (RCOS, rcos[:]), (RSIN, rsin[:])):
        C.chain_dma(qs, s_ld, sb_, ap_)
    qt.wait_ge(s_ld.h, s_ld.n)
    qv.wait_ge(s_ld.h, s_ld.n)

    stn = {}
    evn = {}
    base_pe = s_pe.n
    for ot in range(48):
        sl = ot % 2
        if ot >= 2:
            qg.wait_ge(s_pe.h, base_pe + ot - 1)
        wv = s_w[sl].inc(qg.dma_start(out=wsb[:, sl * 4096:(sl + 1) * 4096], in_=WQKV[ot]), 16)
        qt.wait_ge(s_w[sl].h, wv)
        psl = sl * TLOC
        if ot >= 2:
            qt.wait_ge(s_ev.h, evn[ot - 2])
        for dc in range(32):
            for th in range(2):
                call = qt.matmul(psM[:, psl + th * 512: psl + (th + 1) * 512],
                                 wsb[:, sl * 4096 + dc * 128: sl * 4096 + (dc + 1) * 128],
                                 XBF[:, dc * TLOC + th * 512: dc * TLOC + (th + 1) * 512],
                                 start=(dc == 0), stop=(dc == 31))
        s_pe.inc(call, 1)
        qv.wait_ge(s_pe.h, s_pe.n)
        if ot >= 2:
            qv.wait_ge(s_st[sl].h, stn[ot - 2])
        if ot < 40:
            ev = s_ev.inc(qv.tensor_copy(qtile[:, sl * TLOC:(sl + 1) * TLOC], psM[:, psl:psl + TLOC]), 1)
            qt.wait_ge(s_ev.h, ev)
            for th in range(2):
                call = qt.matmul(psM[:, 2 * TLOC + th * 512:2 * TLOC + (th + 1) * 512],
                                 jrot[:], qtile[:, sl * TLOC + th * 512: sl * TLOC + (th + 1) * 512],
                                 start=True, stop=True)
            pj = s_pj.inc(call, 1)
            qv.wait_ge(s_pj.h, pj)
            qv.tensor_tensor(rop1[:], qtile[:, sl * TLOC:(sl + 1) * TLOC], rcos[:], ALU.mult)
            qv.tensor_tensor(rop2[:], psM[:, 2 * TLOC:3 * TLOC], rsin[:], ALU.mult)
            ev = s_ev.inc(qv.tensor_tensor(rout[:, sl * TLOC:(sl + 1) * TLOC], rop1[:], rop2[:], ALU.add), 1)
            evn[ot] = ev
            qs.wait_ge(s_ev.h, ev)
            if ot < 32:
                stn[ot] = s_st[sl].inc(qs.dma_start(out=QROPE[ot * 128:(ot + 1) * 128, :],
                                                    in_=rout[:, sl * TLOC:(sl + 1) * TLOC]), 16)
            else:
                stn[ot] = s_st[sl].inc(qs.dma_start(out=KV_in[(ot - 32) * 128:(ot - 31) * 128, :],
                                                    in_=rout[:, sl * TLOC:(sl + 1) * TLOC]), 16)
        else:
            ev = s_ev.inc(qv.tensor_copy(rout[:, sl * TLOC:(sl + 1) * TLOC], psM[:, psl:psl + TLOC]), 1)
            evn[ot] = ev
            qs.wait_ge(s_ev.h, ev)
            stn[ot] = s_st[sl].inc(qs.dma_start(out=KV_in[(ot - 32) * 128:(ot - 31) * 128, :],
                                                in_=rout[:, sl * TLOC:(sl + 1) * TLOC]), 16)
    qs.wait_ge(s_st[0].h, s_st[0].n)
    qs.wait_ge(s_st[1].h, s_st[1].n)
    qg.wait_ge(s_st[0].h, s_st[0].n)
    qg.wait_ge(s_st[1].h, s_st[1].n)
    emit_collective(C, "KV")
    emit_collective(C, "QG")


def emit_p6a(C, G):
    qs, qg, qv, qc, qt = (C.q[e] for e in ENGINES)
    D = DRAM_SPEC
    KV_out_h = C.dram_t("KV_out", *D["KV_out"])
    Q_out_h = C.dram_t("Q_out", *D["Q_out"])
    O_in = C.dram_t("O_in", *D["O_in"]).ap()
    DYN6 = C.dram_t("DYN6", *D["DYN6"]).ap()
    MASKD = C.dram_t("MASKD", *D["MASKD"]).ap()
    IDBFd = C.dram_t("IDBF", *D["IDBF"]).ap()
    ones_col = G["ones_col"]
    ones_row_f = G["ones_row_f"]

    ksb = C.sbuf("p6_k", [128, 2 * 4096], BF16)
    vsb = C.sbuf("p6_v", [128, 2 * 4096], BF16)
    vtok = C.sbuf("p6_vt", [128, 2 * 4096], BF16)
    qall = C.sbuf("p6_q", [128, 4 * 8192], BF16)
    ksq = C.sbuf("p6_ksq", [128, 4096], BF16)
    qsq = C.sbuf("p6_qsq", [128, 2 * 512], BF16)
    pbuf = C.sbuf("p6_p", [128, 2 * 512], BF16)
    osb = C.sbuf("p6_o", [128, 2 * 512], BF16)
    recbc = C.sbuf("p6_rb", [128, 512], F32)
    zrec = C.sbuf("p6_zr", [1, 512], F32)
    srow = C.sbuf("p6_srow", [1, 32], F32)
    negmc = C.sbuf("p6_negm", [128, 2], F32)
    idb = C.sbuf("p6_idb", [128, 128], BF16)
    maskd = C.sbuf("p6_maskd", [128, 4 * 512], BF16)
    dynsb = C.sbuf("p6_dyn", [1, 160], I32)

    psS = C.psum("p6_psS", [128, 1024])
    psO = C.psum("p6_psO", [128, 512])
    psZ = C.psum("p6_psZ", [1, 512])
    psB = C.psum("p6_psB", [128, 512])
    psT = C.psum("p6_psT", [128, 1024], BF16)
    psN = [C.psum("p6_psN0", [1, 512]), C.psum("p6_psN1", [1, 512])]

    s_ld = C.sem("p6_ld")
    s_kv = C.sem("p6_kv")
    s_S = C.sem("p6_S")
    s_E = C.sem("p6_E")
    s_pv = C.sem("p6_pv")
    s_tt = C.sem("p6_tt")
    s_vv = C.sem("p6_vv")
    s_cc = C.sem("p6_cc")
    s_ev = C.sem("p6_ev")
    s_st = [C.sem("p6_st0"), C.sem("p6_st1")]

    ccK = C.sems.get("cc_KV")
    if ccK is not None:
        qs.wait_ge(ccK.h, ccK.n)
        qs.wait_ge(C.sems["cc_QG"].h, C.sems["cc_QG"].n)
    C.chain_dma(qs, s_ld, idb[:], IDBFd)
    C.chain_dma(qs, s_ld, dynsb[:], DYN6)
    C.chain_dma(qs, s_ld, maskd[:].rearrange("p (a b) -> p a b", b=512),
                MASKD.rearrange("a p b -> p a b"))
    qs.wait_ge(s_ld.h, s_ld.n)
    _reg = {}

    def _alloc_reg(eng):
        _reg["r"] = eng.register("p6adynr").__enter__()
    qs.raw(_alloc_reg)

    def dyn_dma(sem, out_ap, src_h, idx, pattern):
        def fn(eng, out_ap=out_ap, src_h=src_h, idx=idx, pattern=pattern, semh=sem.h):
            r = _reg["r"]
            eng.reg_load(r, dynsb[:1, idx:idx + 1])
            eng.dma_start(out=out_ap, in_=bass.AP(src_h, r, pattern)).then_inc(semh, 16)
        qs.raw(fn, sim_incs=[(sem.h, 16)])
        sem.n += 16

    for r in range(4):
        dyn_dma(s_kv, ksb[:].rearrange("p (g t) -> p g t", g=2)[:, :, r * TLOC:(r + 1) * TLOC],
                KV_out_h, r, [[TLOC, 128], [128 * TLOC, 2], [1, TLOC]])
        dyn_dma(s_kv, vsb[:].rearrange("p (g t) -> p g t", g=2)[:, :, r * TLOC:(r + 1) * TLOC],
                KV_out_h, 4 + r, [[TLOC, 128], [128 * TLOC, 2], [1, TLOC]])
        dyn_dma(s_kv, qall[:, r * 8192:(r + 1) * 8192].rearrange("p (a t) -> p a t", t=TLOC),
                Q_out_h, 8 + r, [[TLOC, 128], [128 * TLOC, 8], [1, TLOC]])
    kvn = s_kv.n
    for q_ in (qt, qv, qc):
        q_.wait_ge(s_kv.h, kvn)
        q_.wait_ge(s_ld.h, s_ld.n)

    vt_ev = None
    for g in range(2):
        for grp in range(4):
            if vt_ev is not None:
                qt.wait_ge(s_vv.h, vt_ev)
            for i8 in range(8):
                kt8 = grp * 8 + i8
                call = qt.transpose(psT[:, i8 * 128:(i8 + 1) * 128],
                                    vsb[:, g * 4096 + kt8 * 128: g * 4096 + (kt8 + 1) * 128],
                                    idb[:])
            tn = s_tt.inc(call, 1)
            qv.wait_ge(s_tt.h, tn)
            vt_ev = s_vv.inc(qv.tensor_copy(
                vtok[:, g * 4096 + grp * 1024: g * 4096 + (grp + 1) * 1024], psT[:, 0:1024]), 1)

    psn_free = [0, 0]
    for g in range(2):
        sqn = s_cc.inc(qc.activation(ksq[:], ksb[:, g * 4096:(g + 1) * 4096], AF.Square), 1)
        for i in range(8):
            bk = i % 2
            qt.wait_ge(s_cc.h, sqn)
            qt.wait_ge(s_vv.h, psn_free[bk])
            tn = s_tt.inc(qt.matmul(psN[bk][0:1, :], ones_col[:],
                                    ksq[:, i * 512:(i + 1) * 512], start=True, stop=True), 1)
            qv.wait_ge(s_tt.h, tn)
            psn_free[bk] = s_vv.inc(qv.reduce_max(srow[0:1, 8 + i:9 + i], psN[bk][0:1, :], AX.X), 1)
        qv.wait_ge(s_vv.h, s_vv.n)
        mk = s_vv.inc(qv.reduce_max(srow[0:1, g:g + 1], srow[0:1, 8:16], AX.X), 1)
        qc.wait_ge(s_vv.h, mk)
        s_cc.inc(qc.activation(srow[0:1, 2 + g:3 + g], srow[0:1, g:g + 1], AF.Ln), 1)
        if g == 0:
            qc.wait_ge(s_cc.h, s_cc.n)
            qt.wait_ge(s_cc.h, s_cc.n)

    stile, etile, pvend, evend, evrec, negq = {}, {}, {}, {}, {}, {}
    qnmm, stn = {}, {}
    for h in range(8):
        g = h // 4
        for s8 in range(8):
            gi = h * 8 + s8
            ssl = gi % 2
            qb = 16 + 8 * ssl
            nkt = 4 * (s8 + 1)
            qtile = qall[:, (s8 // 2) * 8192 + h * TLOC + (s8 % 2) * 512:
                         (s8 // 2) * 8192 + h * TLOC + (s8 % 2) * 512 + 512]
            if gi >= 2:
                qc.wait_ge(s_tt.h, qnmm[gi - 2])
            sq2 = s_cc.inc(qc.activation(qsq[:, ssl * 512:(ssl + 1) * 512], qtile, AF.Square), 1)
            qt.wait_ge(s_cc.h, sq2)
            qt.wait_ge(s_vv.h, psn_free[ssl])
            qnmm[gi] = s_tt.inc(qt.matmul(psN[ssl][0:1, :], ones_col[:],
                                          qsq[:, ssl * 512:(ssl + 1) * 512], start=True, stop=True), 1)
            qv.wait_ge(s_tt.h, qnmm[gi])
            psn_free[ssl] = s_vv.inc(qv.reduce_max(srow[0:1, qb:qb + 1], psN[ssl][0:1, :], AX.X), 1)
            qc.wait_ge(s_vv.h, psn_free[ssl])
            v1 = s_cc.inc(qc.activation(srow[0:1, qb + 1:qb + 2], srow[0:1, qb:qb + 1], AF.Ln), 1)
            qv.wait_ge(s_cc.h, v1)
            v2 = s_vv.inc(qv.tensor_tensor(srow[0:1, qb + 2:qb + 3], srow[0:1, qb + 1:qb + 2],
                                           srow[0:1, 2 + g:3 + g], ALU.add), 1)
            qc.wait_ge(s_vv.h, v2)
            v3 = s_cc.inc(qc.activation(srow[0:1, qb + 3:qb + 4], srow[0:1, qb + 2:qb + 3],
                                        AF.Exp, scale=0.5, bias=LOGC), 1)
            qv.wait_ge(s_cc.h, v3)
            v4 = s_vv.inc(qv.tensor_scalar_mul(srow[0:1, qb + 4:qb + 5], srow[0:1, qb + 3:qb + 4], -1.0), 1)
            qt.wait_ge(s_vv.h, v4)
            if gi >= 1:
                qt.wait_ge(s_ev.h, evrec[gi - 1])
            tb = s_tt.inc(qt.matmul(psB[:, 0:1], ones_row_f[:], srow[0:1, qb + 4:qb + 5],
                                    start=True, stop=True), 1)
            qv.wait_ge(s_tt.h, tb)
            if gi >= 2:
                qv.wait_ge(s_E.h, etile[gi - 2])
            negq[gi] = s_vv.inc(qv.tensor_copy(negmc[:, ssl:ssl + 1], psB[:, 0:1]), 1)
            ebase = s_E.n
            sbase = s_S.n
            pvk = {}
            for kt in range(nkt):
                sS = kt % 2
                if kt >= 2:
                    qt.wait_ge(s_E.h, ebase + kt - 1)
                masked = (kt >= 4 * s8)
                call = qt.matmul(psS[:, sS * 512:(sS + 1) * 512],
                                 ksb[:, g * 4096 + kt * 128: g * 4096 + (kt + 1) * 128],
                                 qtile, start=True, stop=(not masked))
                if masked:
                    call = qt.matmul(psS[:, sS * 512:(sS + 1) * 512], idb[:],
                                     maskd[:, (kt - 4 * s8) * 512:(kt - 4 * s8 + 1) * 512],
                                     start=False, stop=True)
                s_S.inc(call, 1)
                if kt >= 1:
                    qt.wait_ge(s_E.h, ebase + kt)
                    if kt == 1 and gi >= 1:
                        qt.wait_ge(s_ev.h, evend[gi - 1])
                    pp = pbuf[:, ((kt - 1) % 2) * 512:(((kt - 1) % 2) + 1) * 512]
                    qt.matmul(psZ[0:1, :], ones_col[:], pp, start=(kt == 1), stop=False)
                    call = qt.matmul(psO[:, :],
                                     vtok[:, g * 4096 + (kt - 1) * 128: g * 4096 + kt * 128],
                                     pp, start=(kt == 1), stop=False)
                    pvk[kt - 1] = s_pv.inc(call, 1)
                qc.wait_ge(s_S.h, sbase + kt + 1)
                qc.wait_ge(s_vv.h, negq[gi])
                if kt >= 2:
                    qc.wait_ge(s_pv.h, pvk[kt - 2])
                elif gi >= 1:
                    qc.wait_ge(s_pv.h, pvend[gi - 1])
                call = qc.activation(pbuf[:, sS * 512:(sS + 1) * 512],
                                     psS[:, sS * 512:(sS + 1) * 512],
                                     AF.Exp, scale=SC6, bias=negmc[:, ssl:ssl + 1])
                s_E.inc(call, 1)
            stile[gi] = s_S.n
            etile[gi] = s_E.n
            kt = nkt - 1
            qt.wait_ge(s_E.h, ebase + nkt)
            pp = pbuf[:, (kt % 2) * 512:((kt % 2) + 1) * 512]
            qt.matmul(psZ[0:1, :], ones_col[:], pp, start=False, stop=True)
            call = qt.matmul(psO[:, :], vtok[:, g * 4096 + kt * 128: g * 4096 + (kt + 1) * 128],
                             pp, start=False, stop=True)
            pvend[gi] = s_pv.inc(call, 1)
            qv.wait_ge(s_pv.h, pvend[gi])
            r1 = s_ev.inc(qv.reciprocal(zrec[0:1, :], psZ[0:1, :]), 1)
            qt.wait_ge(s_ev.h, r1)
            tb2 = s_tt.inc(qt.matmul(psB[:, 0:512], ones_row_f[:], zrec[0:1, :], start=True, stop=True), 1)
            qv.wait_ge(s_tt.h, tb2)
            evrec[gi] = s_ev.inc(qv.tensor_copy(recbc[:], psB[:, 0:512]), 1)
            qv.wait_ge(s_ev.h, evrec[gi])
            if gi >= 2:
                qv.wait_ge(s_st[ssl].h, stn[gi - 2])
            evend[gi] = s_ev.inc(qv.tensor_tensor(osb[:, ssl * 512:(ssl + 1) * 512],
                                                  psO[:, :], recbc[:], ALU.mult), 1)
            qs.wait_ge(s_ev.h, evend[gi])
            stn[gi] = s_st[ssl].inc(qs.dma_start(
                out=O_in[h * 128:(h + 1) * 128, s8 * 512:(s8 + 1) * 512],
                in_=osb[:, ssl * 512:(ssl + 1) * 512]), 16)
    for q_ in (qs, qg):
        q_.wait_ge(s_st[0].h, s_st[0].n)
        q_.wait_ge(s_st[1].h, s_st[1].n)
    emit_collective(C, "OG")


def emit_p6b(C, G):
    qs, qg, qv, qc, qt = (C.q[e] for e in ENGINES)
    D = DRAM_SPEC
    O_out_h = C.dram_t("O_out", *D["O_out"])
    WO2 = C.dram_t("wo2T", *D["wo2T"]).ap()
    XRES2 = C.dram_t("XRES2", *D["XRES2"]).ap()
    XRES3 = C.dram_t("XRES3", *D["XRES3"]).ap()
    DYN6 = C.dram_t("DYN6", *D["DYN6"]).ap()

    wsb = C.sbuf("p6b_w", [128, 2 * 512], BF16)
    orh = C.sbuf("p6b_o", [128, 32 * TLOC], BF16)
    x2sb = C.sbuf("p6b_x2", [128, 4 * TLOC], F32)
    x3sb = C.sbuf("p6b_x3", [128, 4 * TLOC], F32)
    dynsb = C.sbuf("p6b_dyn", [1, 160], I32)
    psM = C.psum("p6b_psM", [128, 4 * TLOC])

    s_ld = C.sem("p6b_ld")
    s_w = [C.sem("p6b_w0"), C.sem("p6b_w1")]
    s_o = C.sem("p6b_o")
    s_x = C.sem("p6b_x")
    s_pe = C.sem("p6b_pe")
    s_ev = C.sem("p6b_ev")
    s_st = [C.sem("p6b_st0"), C.sem("p6b_st1")]

    ccO = C.sems.get("cc_OG")
    if ccO is not None:
        qs.wait_ge(ccO.h, ccO.n)
    C.chain_dma(qs, s_ld, dynsb[:], DYN6)
    qs.wait_ge(s_ld.h, s_ld.n)
    _reg6b = {}

    def _alloc_reg6b(eng):
        _reg6b["r"] = eng.register("p6bdynr").__enter__()
    qs.raw(_alloc_reg6b)

    for part in range(4):
        def o_dma(eng, out_ap=orh[:, part * 8 * TLOC:(part + 1) * 8 * TLOC].rearrange(
                      "p (a t) -> p a t", t=TLOC),
                  src_h=O_out_h, idx=12 + part, semh=s_o.h):
            r = _reg6b["r"]
            eng.reg_load(r, dynsb[:1, idx:idx + 1])
            eng.dma_start(out=out_ap,
                          in_=bass.AP(src_h, r, [[4096, 128], [128 * 4096, 8], [1, TLOC]])).then_inc(semh, 16)
        qs.raw(o_dma, sim_incs=[(s_o.h, 16)])
        s_o.n += 16
    qt.wait_ge(s_o.h, s_o.n)

    stn = {}
    for dmg in range(8):
        qs.wait_ge(s_ev.h, s_ev.n)
        for i4 in range(4):
            C.chain_dma(qs, s_x, x2sb[:, i4 * TLOC:(i4 + 1) * TLOC],
                        XRES2[(dmg * 4 + i4) * 128:(dmg * 4 + i4 + 1) * 128, :])
        xldn = s_x.n
        for hc in range(32):
            it = dmg * 32 + hc
            sl = it % 2
            if it >= 2:
                qg.wait_ge(s_pe.h, it - 1)
            wv = s_w[sl].inc(qg.dma_start(out=wsb[:, sl * 512:(sl + 1) * 512], in_=WO2[dmg, hc]), 16)
            qt.wait_ge(s_w[sl].h, wv)
            if hc == 0:
                qt.wait_ge(s_ev.h, s_ev.n)
            for dmc in range(4):
                for th in range(2):
                    call = qt.matmul(psM[:, dmc * TLOC + th * 512: dmc * TLOC + (th + 1) * 512],
                                     wsb[:, sl * 512 + dmc * 128: sl * 512 + (dmc + 1) * 128],
                                     orh[:, hc * TLOC + th * 512: hc * TLOC + (th + 1) * 512],
                                     start=(hc == 0), stop=(hc == 31))
            s_pe.inc(call, 1)
        qv.wait_ge(s_pe.h, s_pe.n)
        qv.wait_ge(s_x.h, xldn)
        for dmc in range(4):
            if dmg >= 1:
                qv.wait_ge(s_st[(dmg - 1) % 2].h, stn[(dmg - 1, dmc)])
            call = qv.tensor_tensor(x3sb[:, dmc * TLOC:(dmc + 1) * TLOC],
                                    psM[:, dmc * TLOC:(dmc + 1) * TLOC],
                                    x2sb[:, dmc * TLOC:(dmc + 1) * TLOC], ALU.add)
            ev = s_ev.inc(call, 1)
            qs.wait_ge(s_ev.h, ev)
            stn[(dmg, dmc)] = s_st[dmg % 2].inc(qs.dma_start(
                out=XRES3[(dmg * 4 + dmc) * 128:(dmg * 4 + dmc + 1) * 128, :],
                in_=x3sb[:, dmc * TLOC:(dmc + 1) * TLOC]), 16)
    qs.wait_ge(s_st[0].h, s_st[0].n)
    qs.wait_ge(s_st[1].h, s_st[1].n)


def emit_p7(C, G):
    qs, qg, qv, qc, qt = (C.q[e] for e in ENGINES)
    D = DRAM_SPEC
    XRES3 = C.dram_t("XRES3", *D["XRES3"]).ap()
    WDQ_out = C.dram_t("WDQ_out", *D["WDQ_out"]).ap()
    P7Z = C.dram_t("P7Z", *D["P7Z"]).ap()
    P7S = C.dram_t("P7S", *D["P7S"]).ap()
    P7SCOL = C.dram_t("P7SCOL", *D["P7SCOL"]).ap()
    P7BCOL = C.dram_t("P7BCOL", *D["P7BCOL"]).ap()
    OUT = C.dram_t("OUT", *D["OUT"]).ap()
    ones_col = G["ones_col"]

    XBF = C.sbuf("p7_xbf", [128, 32 * TLOC], BF16)
    psM = C.psum("p7_psM", [128, 4 * TLOC])
    emit_ln(C, G, XRES3, XBF, psM, "p7ln")

    wsb = C.sbuf("p7_w", [128, 16384], BF16)
    zsrow = C.sbuf("p7_zs", [1, 2 * 4096], F32)
    nzbf = C.sbuf("p7_nz", [1, 4096], BF16)
    rsrow = C.sbuf("p7_rs", [1, TLOC], F32)
    rsbf = C.sbuf("p7_rsb", [1, TLOC], BF16)
    sccol = C.sbuf("p7_sc", [128, 32], F32)
    bcol = C.sbuf("p7_bc", [128, 32], F32)
    x3sb = C.sbuf("p7_x3", [128, 4 * TLOC], F32)
    x4sb = C.sbuf("p7_x4", [128, 4 * TLOC], F32)

    s_ld = C.sem("p7_ld")
    s_tt = C.sem("p7_tt")
    s_vv = C.sem("p7_vv")
    s_w = C.sem("p7_w")
    s_x = C.sem("p7_x")
    s_pe = C.sem("p7_pe")
    s_ev = C.sem("p7_ev")
    s_st = [C.sem("p7_st0"), C.sem("p7_st1")]

    z_ = zsrow[0:1, 0:4096]
    ss_ = zsrow[0:1, 4096:8192]
    C.chain_dma(qs, s_ld, z_, P7Z)
    C.chain_dma(qs, s_ld, ss_, P7S)
    C.chain_dma(qs, s_ld, sccol[:], P7SCOL)
    C.chain_dma(qs, s_ld, bcol[:], P7BCOL)
    qv.wait_ge(s_ld.h, s_ld.n)
    s_vv.inc(qv.reciprocal(ss_, ss_), 1)
    qv.wait_ge(s_vv.h, s_vv.n)
    s_vv.inc(qv.tensor_tensor(z_, z_, ss_, ALU.mult), 1)
    qv.wait_ge(s_vv.h, s_vv.n)
    s_vv.inc(qv.tensor_scalar_mul(z_, z_, -1.0), 1)
    qv.wait_ge(s_vv.h, s_vv.n)
    nzn = s_vv.inc(qv.tensor_copy(nzbf[:], z_), 1)
    for dc in range(32):
        for th in range(2):
            call = qt.matmul(psM[0:1, th * 512:(th + 1) * 512], ones_col[:],
                             XBF[:, dc * TLOC + th * 512: dc * TLOC + (th + 1) * 512],
                             start=(dc == 0), stop=(dc == 31))
    rsn = s_tt.inc(call, 1)
    qv.wait_ge(s_tt.h, rsn)
    s_vv.inc(qv.tensor_copy(rsrow[:], psM[0:1, 0:TLOC]), 1)
    qv.wait_ge(s_vv.h, s_vv.n)
    rsv = s_vv.inc(qv.tensor_copy(rsbf[:], rsrow[:]), 1)
    qt.wait_ge(s_vv.h, rsv)

    stn = {}
    for ocg in range(8):
        if ocg >= 1:
            qg.wait_ge(s_pe.h, ocg)
        for part in range(4):
            a0 = part * 8
            C.chain_dma(qg, s_w, wsb[:, a0 * 512:(a0 + 8) * 512].rearrange("p (a o) -> p a o", o=512),
                        WDQ_out.rearrange("(a p) o -> p a o", p=128)[:, a0:a0 + 8, ocg * 512:(ocg + 1) * 512])
        wn = s_w.n
        qs.wait_ge(s_ev.h, s_ev.n)
        for oc in range(4):
            C.chain_dma(qs, s_x, x3sb[:, oc * TLOC:(oc + 1) * TLOC],
                        XRES3[(ocg * 4 + oc) * 128:(ocg * 4 + oc + 1) * 128, :])
        xn = s_x.n
        qt.wait_ge(s_w.h, wn)
        qt.wait_ge(s_ev.h, s_ev.n)
        for ic in range(32):
            for oc in range(4):
                for th in range(2):
                    qt.matmul(psM[:, oc * TLOC + th * 512: oc * TLOC + (th + 1) * 512],
                              wsb[:, ic * 512 + oc * 128: ic * 512 + (oc + 1) * 128],
                              XBF[:, ic * TLOC + th * 512: ic * TLOC + (th + 1) * 512],
                              start=(ic == 0), stop=False)
        for oc in range(4):
            for th in range(2):
                call = qt.matmul(psM[:, oc * TLOC + th * 512: oc * TLOC + (th + 1) * 512],
                                 nzbf[0:1, ocg * 512 + oc * 128: ocg * 512 + (oc + 1) * 128],
                                 rsbf[0:1, th * 512:(th + 1) * 512], start=False, stop=True)
        s_pe.inc(call, 1)
        qv.wait_ge(s_pe.h, s_pe.n)
        qv.wait_ge(s_x.h, xn)
        for oc in range(4):
            if ocg >= 1:
                qv.wait_ge(s_st[(ocg - 1) % 2].h, stn[(ocg - 1, oc)])
            qv.tensor_scalar(x4sb[:, oc * TLOC:(oc + 1) * TLOC], psM[:, oc * TLOC:(oc + 1) * TLOC],
                             sccol[:, ocg * 4 + oc: ocg * 4 + oc + 1],
                             bcol[:, ocg * 4 + oc: ocg * 4 + oc + 1], ALU.mult, ALU.add)
            call = qv.tensor_tensor(x4sb[:, oc * TLOC:(oc + 1) * TLOC],
                                    x4sb[:, oc * TLOC:(oc + 1) * TLOC],
                                    x3sb[:, oc * TLOC:(oc + 1) * TLOC], ALU.add)
            ev = s_ev.inc(call, 1)
            qs.wait_ge(s_ev.h, ev)
            stn[(ocg, oc)] = s_st[ocg % 2].inc(qs.dma_start(
                out=OUT[(ocg * 4 + oc) * 128:(ocg * 4 + oc + 1) * 128, :],
                in_=x4sb[:, oc * TLOC:(oc + 1) * TLOC]), 16)
    qs.wait_ge(s_st[0].h, s_st[0].n)
    qs.wait_ge(s_st[1].h, s_st[1].n)



def emit_p7ln_dbg(C, G):
    qs, qg, qv, qc, qt = (C.q[e] for e in ENGINES)
    D = DRAM_SPEC
    XRES3 = C.dram_t("XRES3", *D["XRES3"]).ap()
    XBFP = C.dram_t("XBFP", *D["XBFP"]).ap()
    STATP = C.dram_t("STATP", *D["STATP"]).ap()
    XBF = C.sbuf("p7_xbf", [128, 32 * TLOC], BF16)
    psM = C.psum("p7_psM", [128, 4 * TLOC])
    emit_ln(C, G, XRES3, XBF, psM, "p7ln")
    s_st = C.sem("dbg_st")
    stat = C.sbufs_dbg["p7ln_stat"]
    ab = C.sbufs_dbg["p7ln_ab"]
    for c in range(32):
        C.chain_dma(qs, s_st, XBFP.rearrange("(a p) t -> a p t", p=128)[c], XBF[:, c * TLOC:(c + 1) * TLOC])
    C.chain_dma(qs, s_st, STATP[0:1, :], stat[0:1, 0:3 * TLOC])
    C.chain_dma(qs, s_st, STATP[1:2, 0:TLOC], ab[0:1, 0:TLOC])
    C.chain_dma(qs, s_st, STATP[1:2, TLOC:2 * TLOC], ab[0:1, TLOC:2 * TLOC])
    qs.wait_ge(s_st.h, s_st.n)


def make_globals(C):
    G = {}
    ones_col = C.sbuf("g_ones_col", [128, 1], BF16)
    ones_row_f = C.sbuf("g_ones_row_f", [1, 128], F32)
    ones_row_b = C.sbuf("g_ones_row_b", [1, 128], BF16)
    C.q["vector"].memset(ones_col[:], 1.0)
    C.q["vector"].memset(ones_row_f[:], 1.0)
    C.q["vector"].memset(ones_row_b[:], 1.0)
    cst = C.sbuf("g_cst", [128, 8], F32)
    G["cst"] = cst
    for i, v in enumerate(CONST_VALS):
        C.q["vector"].memset(cst[:, i:i + 1], v)
        C.nc.const_aps.register(F32, v, cst[:, i:i + 1])
    s = C.sem("g_init")
    C.q["vector"].drain()
    C.q["vector"].sem_inc(s.h, 1)
    s.n = 1
    for e in ("tensor", "scalar", "gpsimd", "sync"):
        C.q[e].wait_ge(s.h, 1)
    G["ones_col"] = ones_col
    G["ones_row_f"] = ones_row_f
    G["ones_row_b"] = ones_row_b
    return G


PHASES = {
    "P1": emit_p1,
    "P2a": emit_p2a,
    "P2b": emit_p2b,
    "P3": emit_p3,
    "P4": emit_p4,
    "P5": emit_p5,
    "P6a": emit_p6a,
    "P6b": emit_p6b,
    "P7": emit_p7,
    "P7LN": emit_p7ln_dbg,
}

LAUNCH_PLAN = [["P1"], ["P2a"], ["P2b"], ["P3"], ["P4"], ["P5"], ["P6a"], ["P6b"], ["P7"]]
SPLIT = True

PRODUCES = {
    "P1": ["ZX", "DT", "HALO_in", "WDQ_in", "LN0AB"],
    "P2a": ["XCPARK", "SLOC", "YD", "ACS", "GLOG", "SST_in", "GAM_in"],
    "P2b": ["YBUF", "RMS", "DBG"],
    "P3": ["XRES"],
    "P4": ["XRES2", "CROW"],
    "P5": ["QROPE", "KV_in"],
    "P6a": ["O_in"],
    "P6b": ["XRES3"],
    "P7": ["OUT"],
    "P7LN": ["XBFP", "STATP"],
}
CONSUMES = {
    "P1": ["x0", "wip", "qw_sh"],
    "P2a": ["ZX", "DT", "HALO_out", "cw", "cb", "dtb_col", "alog_col", "d_col",
            "U01", "GTMASK", "PARITY", "PAT2", "IDF32", "IDBF", "HGATE", "DYNOFF"],
    "P2b": ["ZX", "XCPARK", "SLOC", "YD", "ACS", "GLOG", "SST_out", "GAM_out",
            "WMSKT", "WVALID", "SPLIT2", "PAT2", "PARITY", "IDBF", "nw_cols"],
    "P3": ["YBUF", "RMS", "wot", "x0", "LN0AB"],
    "P4": ["XRES", "routerT", "AT", "wbaseT", "BT4", "baseb", "IDF32"],
    "P5": ["XRES2", "wqkvT", "JROT", "RCOS", "RSIN"],
    "P6a": ["KV_out", "Q_out", "DYN6", "MASKD", "IDBF"],
    "P6b": ["O_out", "wo2T", "XRES2", "DYN6"],
    "P7": ["XRES3", "WDQ_out", "P7Z", "P7S", "P7SCOL", "P7BCOL"],
    "P7LN": ["XRES3"],
}
CC_SITE = {"P1": ["HALO", "WDQ"], "P2a": ["SST", "GAM"], "P5": ["KV", "QG"], "P6a": ["OG"]}


def build_launch(phases, split_mode, detect_races=True):
    nc = bass.Bass("TRN2", target_bir_lowering=False, detect_race_conditions=detect_races)
    produced_here = set(sum((PRODUCES.get(p, []) for p in phases), []))
    mode_tensors = {}
    for p in phases:
        for tname in CONSUMES.get(p, []):
            if tname not in produced_here:
                mode_tensors[tname] = "ExternalInput"
        for tname in PRODUCES.get(p, []):
            mode_tensors[tname] = "ExternalOutput" if split_mode else "Internal"
    if "OUT" in mode_tensors:
        mode_tensors["OUT"] = "ExternalOutput"
    with ExitStack() as ctx:
        C = Ctx(nc, ctx, mode_tensors, split=split_mode)
        G = make_globals(C)
        for i, p in enumerate(phases):
            PHASES[p](C, G)
            if not split_mode:
                pass
            if i < len(phases) - 1:
                C.barrier()
        C.barrier()
        C.simulate()
        C.finish()
    return nc


_PERM = np.concatenate([np.arange(0, 128, 2), np.arange(1, 128, 2)])


def prep_shared(inputs):
    S = {}
    W = np.ascontiguousarray(inputs["L0_attn_in_proj_w"])
    Wd = W[18432:][_PERM]
    W = np.concatenate([W[:18432], Wd], axis=0)
    S["wip"] = np.ascontiguousarray(
        W.reshape(145, 128, 32, 128).transpose(0, 3, 2, 1))
    S["cw"] = np.ascontiguousarray(
        inputs["L0_attn_conv_w"].reshape(80, 128, 4).transpose(1, 0, 2)).astype(np.float32)
    S["cb"] = np.ascontiguousarray(inputs["L0_attn_conv_b"].reshape(80, 128).T).astype(np.float32)
    S["dtb_col"] = np.ascontiguousarray(inputs["L0_attn_dt_bias"][_PERM].reshape(128, 1)).astype(np.float32)
    S["alog_col"] = np.ascontiguousarray(inputs["L0_attn_A_log"][_PERM].reshape(128, 1)).astype(np.float32)
    S["d_col"] = np.ascontiguousarray(inputs["L0_attn_D"][_PERM].reshape(128, 1)).astype(np.float32)
    S["nw_cols"] = np.ascontiguousarray(inputs["L0_attn_norm_w"].reshape(64, 128).T).astype(np.float32)
    sl = np.arange(128)[:, None]
    ll = np.arange(256)[None, :]
    S["U01"] = np.stack([(sl <= ll), (sl + 128 <= ll)]).astype(np.float32)
    S["GTMASK"] = np.stack([(ll >= sl), (ll >= sl + 128)]).astype(BF16NP)
    sp2 = np.zeros((2, 128), np.float32)
    sp2[0, :64] = 1.0
    sp2[1, 64:] = 1.0
    S["SPLIT2"] = sp2
    hh = np.arange(128)[:, None]
    pp = np.arange(128)[None, :]
    S["PARITY"] = (((pp < 64) & (hh < 64)) | ((pp >= 64) & (hh >= 64))).astype(np.float32)
    S["PAT2"] = (hh % 64 == np.arange(64)[None, :]).astype(np.float32)
    S["IDF32"] = np.eye(128, dtype=np.float32)
    S["IDBF"] = np.eye(128).astype(BF16NP)
    W2 = np.ascontiguousarray(inputs["L0_attn_out_proj_w"].T)
    S["wot"] = np.ascontiguousarray(
        W2.reshape(64, 128, 8, 512).transpose(2, 0, 1, 3))
    S["routerT"] = np.ascontiguousarray(
        inputs["L0_ffn_router_w"].T.reshape(32, 128, 16)).astype(np.float32)
    S["AT"] = np.ascontiguousarray(
        inputs["L0_ffn_A"].T.reshape(32, 128, 64)).astype(np.float32)
    W3 = np.ascontiguousarray(inputs["L0_ffn_base_w"].T)
    S["wbaseT"] = np.ascontiguousarray(
        W3.reshape(32, 128, 32, 128).transpose(2, 1, 0, 3))
    BT = inputs["L0_ffn_B"].transpose(0, 2, 1)
    S["BT4"] = np.ascontiguousarray(
        BT.reshape(16, 64, 32, 128).transpose(2, 1, 0, 3).reshape(32, 64, 2048))
    S["baseb"] = np.ascontiguousarray(inputs["L0_ffn_base_b"].reshape(32, 128).T).astype(np.float32)
    W4 = np.ascontiguousarray(inputs["L1_attn_w_qkv"])
    S["wqkvT"] = np.ascontiguousarray(
        W4.reshape(32, 128, 48, 128).transpose(2, 1, 0, 3))
    J = np.zeros((128, 128), np.float32)
    for i in range(64):
        J[2 * i, 2 * i + 1] = 1.0
        J[2 * i + 1, 2 * i] = 1.0
    S["JROT"] = J.astype(BF16NP)
    mkd = np.zeros((4, 128, 512), np.float32)
    klg = np.arange(128)[:, None]
    qlg = np.arange(512)[None, :]
    for dk in range(4):
        mkd[dk] = np.where(dk * 128 + klg <= qlg, 0.0, -1e6)
    S["MASKD"] = mkd.astype(BF16NP)
    W5 = np.ascontiguousarray(inputs["L1_attn_w_o"])
    S["wo2T"] = np.ascontiguousarray(W5.reshape(32, 128, 8, 512).transpose(2, 0, 1, 3))
    S["P7Z"] = np.ascontiguousarray(inputs["L1_ffn_zeros"][:, 0][None, :]).astype(np.float32)
    S["P7S"] = np.ascontiguousarray(inputs["L1_ffn_scales"][:, 0][None, :]).astype(np.float32)
    S["P7SCOL"] = np.ascontiguousarray(inputs["L1_ffn_scales"][:, 0].reshape(32, 128).T).astype(np.float32)
    S["P7BCOL"] = np.ascontiguousarray(inputs["L1_ffn_bias"].reshape(32, 128).T).astype(np.float32)
    return S


def prep_core(inputs, S, c):
    b, j = c // 4, c % 4
    d = {}
    x = inputs["L0_attn_hidden_states"][b, j * TLOC:(j + 1) * TLOC, :]
    d["x0"] = np.ascontiguousarray(x.T)
    d["wip"] = S["wip"]
    d["qw_sh"] = np.ascontiguousarray(inputs["L1_ffn_qweight"][64 * c:64 * (c + 1)])
    for kcp in S:
        if kcp not in d:
            d[kcp] = S[kcp]
    wm = np.zeros((4, 4), np.float32)
    for kk in range(4):
        for src in range(4):
            wm[kk, src] = 1.0 if src < kk < j else 0.0
    d["WMSKT"] = wm
    d["WVALID"] = np.broadcast_to((np.arange(4)[None, :] < j).astype(np.float32), (128, 4)).copy()
    d["HGATE"] = np.full((128, 1), 1.0 if j > 0 else 0.0, np.float32)
    inv = 1.0 / 10000.0 ** (np.arange(0, 128, 2, dtype=np.float64) / 128)
    tg = np.arange(j * TLOC, (j + 1) * TLOC, dtype=np.float64)
    ang = tg[None, :] * inv[:, None]
    cos = np.cos(ang)
    sin = np.sin(ang)
    rc = np.zeros((128, TLOC), np.float32)
    rs = np.zeros((128, TLOC), np.float32)
    rc[0::2] = cos
    rc[1::2] = cos
    rs[0::2] = -sin
    rs[1::2] = sin
    d["RCOS"] = rc
    d["RSIN"] = rs
    dyn = np.zeros((1, 8), np.int32)
    for gg in range(8):
        dyn[0, gg] = max(j - 1, 0) * 128 * 240 + gg * 30
    d["DYNOFF"] = dyn
    dyn6 = np.zeros((1, 160), np.int32)
    for r in range(4):
        dyn6[0, r] = (r * 2048 + j * 256) * TLOC
        dyn6[0, 4 + r] = (r * 2048 + 1024 + j * 256) * TLOC
        dyn6[0, 8 + r] = (r * 4096 + j * 1024) * TLOC
    for part in range(4):
        dyn6[0, 12 + part] = part * 8 * 128 * 4096 + j * TLOC
    d["DYN6"] = dyn6
    return d


def run_all(inputs):
    S = prep_shared(inputs)
    core_inputs = [prep_core(inputs, S, c) for c in range(NCORE)]
    state = [dict(ci) for ci in core_inputs]
    split = SPLIT
    for phases in LAUNCH_PLAN:
        nc = build_launch(phases, split)
        needed = set()
        for p in phases:
            needed |= set(CONSUMES.get(p, []))
        in_maps = [{k: st[k] for k in needed if k in st} for st in state]
        res = run_bass_kernel_spmd(nc, in_maps, core_ids=list(range(NCORE)))
        for c in range(NCORE):
            state[c].update(res.results[c])
        if split:
            for p in phases:
                for ccn in CC_SITE.get(p, []):
                    gin, gout, kind = COLLECTIVES[ccn]
                    for grp in _groups(kind):
                        cat = np.concatenate([state[r][gin] for r in grp], axis=0)
                        for r in grp:
                            state[r][gout] = cat
    out = np.zeros((BATCH, SEQ, DM), dtype=np.float32)
    if "OUT" in state[0]:
        for c in range(NCORE):
            b, j = c // 4, c % 4
            out[b, j * TLOC:(j + 1) * TLOC, :] = state[c]["OUT"].T
    return state, out


def kernel(**inputs):
    return run_all(inputs)[1]
